# version 08
# baseline (speedup 1.0000x reference)
import numpy as np
import ml_dtypes
import concourse.bass as bass
import concourse.mybir as mybir
from concourse.bass_utils import run_bass_kernel_spmd

F32 = mybir.dt.float32
F16 = mybir.dt.float16
ALU = mybir.AluOpType
AF = mybir.ActivationFunctionType
AX = mybir.AxisListType

T = 1024
DM = 2048
DIN = 4096
NST = 128
CONVD = DIN + 2 * NST
PROJ = DIN + CONVD + 64
NH = 64
HPD = 64
CH = 128
NCHUNK = T // CH
EPS = 1e-5
KD = DM // 128
NXBC = CONVD // 128

BUILD_UPTO = 99
DEBUG_TAPS = []


class Ctr:
    def __init__(self, stack, nc, name, dma=False):
        self.h = stack.enter_context(nc.semaphore(name))
        self.v = 0
        self.dma = dma


class Sched:

    ENGS = ("sync", "gpsimd", "tensor", "vector", "scalar")

    def __init__(self, nc):
        self.nc = nc
        self.q = {e: [] for e in self.ENGS}

    def on(self, eng, fn):
        self.q[eng].append(fn)

    def wait(self, eng, ctr, val):
        if ctr.dma:
            val = ctr.v
        if val > 0:
            self.q[eng].append(lambda e, h=ctr.h, v=val: e.wait_ge(h, v))

    def op(self, eng, fn, ctr=None, n=1):
        if ctr is None:
            self.q[eng].append(lambda e: fn(e))
            return None
        ctr.v += n
        self.q[eng].append(lambda e, h=ctr.h, k=n: fn(e).then_inc(h, k))
        return ctr.v

    def dma(self, eng, ctr, out, in_, **kw):
        ctr.v += 16
        self.q[eng].append(
            lambda e, h=ctr.h: e.dma_start(out=out, in_=in_, **kw).then_inc(h, 16))
        return ctr.v

    def emit(self, name=""):
        from contextlib import nullcontext
        scope = self.nc.named_scope(name) if name else nullcontext()
        with scope, self.nc.Block() as blk:
            for eng in self.ENGS:
                ops = self.q[eng]
                if not ops:
                    continue

                def body(e, ops=ops):
                    for f in ops:
                        f(e)
                getattr(blk, eng)(body)
        self.q = {e: [] for e in self.ENGS}


def _bcast_hp(tile_ap, nh, np_):
    return tile_ap.unsqueeze(2).broadcast_to([128, nh, np_])


def _bcast_outer(tile_ap, nouter):
    return tile_ap.unsqueeze(1).broadcast_to([128, nouter, tile_ap.shape[1]])


def build():
    from contextlib import ExitStack

    nc = bass.Bass("TRN2", target_bir_lowering=False,
                   detect_race_conditions=False)

    x_in = nc.dram_tensor("x_in", [T, DM], F32, kind="ExternalInput")
    w_in_T = nc.dram_tensor("w_in_T", [DM, PROJ], F32, kind="ExternalInput")
    w_out_T = nc.dram_tensor("w_out_T", [DIN, DM], F32, kind="ExternalInput")
    w0_T = nc.dram_tensor("w0_T", [DM, 8192], F32, kind="ExternalInput")
    w1_T = nc.dram_tensor("w1_T", [8192, DM], F32, kind="ExternalInput")
    conv_w = nc.dram_tensor("conv_w", [CONVD, 4], F32, kind="ExternalInput")
    conv_b = nc.dram_tensor("conv_b", [CONVD], F32, kind="ExternalInput")
    dt_bias = nc.dram_tensor("dt_bias", [NH], F32, kind="ExternalInput")
    A_log = nc.dram_tensor("A_log", [NH], F32, kind="ExternalInput")
    Dp = nc.dram_tensor("Dp", [NH], F32, kind="ExternalInput")
    norm_w = nc.dram_tensor("norm_w", [DIN], F32, kind="ExternalInput")
    b0 = nc.dram_tensor("b0", [8192], F32, kind="ExternalInput")
    b1 = nc.dram_tensor("b1", [DM], F32, kind="ExternalInput")
    flags = nc.dram_tensor("flags", [2], F32, kind="ExternalInput")
    y_out = nc.dram_tensor("y_out", [T, DM], F32, kind="ExternalOutput")

    xn_st = nc.dram_tensor("xn_st", [T, DM], F16)
    act_st = nc.dram_tensor("act_st", [CONVD, T], F16)
    gate_st = nc.dram_tensor("gate_st", [DIN, T], F16)
    dAc_st = nc.dram_tensor("dAc_st", [NH, T], F32)
    dasum_st = nc.dram_tensor("dasum_st", [NCHUNK, NH], F16)
    tmq_st = nc.dram_tensor("tmq_st", [4 * NH, T], F16)
    yn_st = nc.dram_tensor("yn_st", [T, DIN], F16)
    attn_st = nc.dram_tensor("attn_st", [DM, T], F16)
    x2_st = nc.dram_tensor("x2_st", [T, DM], F32)
    x2n_st = nc.dram_tensor("x2n_st", [T, DM], F16)
    e_st = nc.dram_tensor("e_st", [DM, T], F16)
    cc1_in = nc.dram_tensor("cc1_in", [CONVD, 3], F32)
    cc1_out = nc.dram_tensor("cc1_out", [CONVD, 3], F32)
    cc2_in = nc.dram_tensor("cc2_in", [NST, DIN], F32)
    cc2_out = nc.dram_tensor("cc2_out", [NST, DIN], F32)

    taps = {}
    tapshape = {
        "t_proj": [PROJ, T], "t_act": [CONVD, T], "t_dt": [NH, T],
        "t_dAc": [NH, T], "t_y": [T, DIN], "t_yn": [T, DIN],
        "t_attn": [DM, T], "t_x2": [T, DM], "t_S": [NST, DIN],
        "t_xn": [T, DM], "t_gate": [DIN, T], "t_x2n": [T, DM],
        "t_e": [DM, T],
    }
    for name in DEBUG_TAPS:
        if name in tapshape:
            taps[name] = nc.dram_tensor(name, tapshape[name], F32, kind="ExternalOutput")

    es = ExitStack()

    flagA = nc.alloc_sbuf_tensor("flagA", [128, 1], F32)
    flagB = nc.alloc_sbuf_tensor("flagB", [128, 1], F32)
    A_col = nc.alloc_sbuf_tensor("A_col", [NH, 1], F32)
    dtb_col = nc.alloc_sbuf_tensor("dtb_col", [NH, 1], F32)
    D_rep = nc.alloc_sbuf_tensor("D_rep", [128, NH], F32)
    cw_cols = nc.alloc_sbuf_tensor("cw_cols", [128, NXBC * 4], F32)
    cb_cols = nc.alloc_sbuf_tensor("cb_cols", [128, NXBC], F32)
    b0_cols = nc.alloc_sbuf_tensor("b0_cols", [128, 64], F32)
    b1_cols = nc.alloc_sbuf_tensor("b1_cols", [128, 16], F32)
    eps_col = nc.alloc_sbuf_tensor("eps_col", [128, 1], F32)
    tril01 = nc.alloc_sbuf_tensor("tril01", [128, 128], F16)
    ident32 = nc.alloc_sbuf_tensor("ident32", [128, 128], F32)
    ones64 = nc.alloc_sbuf_tensor("ones64", [NH, T], F32)
    stash_first = nc.alloc_sbuf_tensor("stash_first", [128, NXBC * 3], F16)
    stash_halo = nc.alloc_sbuf_tensor("stash_halo", [128, NXBC * 3], F32)
    fixpack = nc.alloc_sbuf_tensor("fixpack", [128, NXBC * 3], F32)
    fixp16 = nc.alloc_sbuf_tensor("fixp16", [128, NXBC * 3], F16)

    sq = Ctr(es, nc, "sq", dma=True)
    gq = Ctr(es, nc, "gq", dma=True)
    st = Ctr(es, nc, "st", dma=True)
    cq = Ctr(es, nc, "cq", dma=True)
    xq = [Ctr(es, nc, f"xq{i}", dma=True) for i in range(2)]
    rqs = [Ctr(es, nc, f"rqs{i}", dma=True) for i in range(2)]
    bq = Ctr(es, nc, "bq", dma=True)
    gqh = [Ctr(es, nc, f"gqh{i}", dma=True) for i in range(2)]
    gst = [Ctr(es, nc, f"gst{i}", dma=True) for i in range(2)]
    ast = [Ctr(es, nc, f"ast{i}", dma=True) for i in range(2)]
    x2q = Ctr(es, nc, "x2q", dma=True)
    pq = Ctr(es, nc, "pq")
    mm = Ctr(es, nc, "mm")
    vv = Ctr(es, nc, "vv")
    ssc = Ctr(es, nc, "ssc")
    ccs = Ctr(es, nc, "ccs")

    banks = [es.enter_context(nc.psum_tensor(f"pb{i}", [128, 512], F32))
             for i in range(8)]

    S = Sched(nc)


    def emit_dram_tap(tap_name, src_dram, rows, cols, f16=True):
        if tap_name not in taps:
            return
        with nc.sbuf_tensor(tap_name + "_stg", [128, cols], F32) as stg:
            for r in range(rows // 128):
                if f16:
                    S.wait("gpsimd", st, st.v)
                    w_l = S.dma("gpsimd", gq, stg[:, :], src_dram[128 * r:128 * (r + 1), :])
                    S.wait("sync", gq, w_l)
                else:
                    S.wait("sync", st, st.v)
                    w_l = S.dma("sync", sq, stg[:, :], src_dram[128 * r:128 * (r + 1), :])
                    S.wait("sync", sq, w_l)
                S.dma("sync", st, taps[tap_name][128 * r:128 * (r + 1), :], stg[:, :])
            S.wait("sync", st, st.v)
            S.emit("tap_" + tap_name)

    S.dma("sync", cq, flagA[:, :], bass.AP(flags, 0, [[0, 128], [1, 1]]))
    S.dma("sync", cq, flagB[:, :], bass.AP(flags, 1, [[0, 128], [1, 1]]))
    w_acol = S.dma("sync", cq, A_col[:, :], bass.AP(A_log, 0, [[1, NH], [1, 1]]))
    S.dma("sync", cq, dtb_col[:, :], bass.AP(dt_bias, 0, [[1, NH], [1, 1]]))
    S.dma("sync", cq, D_rep[:, :], bass.AP(Dp, 0, [[0, 128], [1, NH]]))
    S.dma("sync", cq, cw_cols[:, :], bass.AP(conv_w, 0, [[4, 128], [512, NXBC], [1, 4]]))
    w_consts = cq.v

    def _f(fn):
        return fn

    S.op("gpsimd", lambda g: g.memset(eps_col[:, :], EPS))
    w_m1_ = S.op("gpsimd", lambda g: g.memset(tril01[:, :], 1.0), pq)
    S.wait("gpsimd", pq, w_m1_)
    S.op("gpsimd", lambda g: g.affine_select(
        out=tril01[:, :], in_=tril01[:, :], compare_op=ALU.is_ge, fill=0.0,
        base=0, pattern=[[1, 128]], channel_multiplier=-1))
    w_m2_ = S.op("gpsimd", lambda g: g.memset(ident32[:, :], 0.0), pq)
    S.wait("gpsimd", pq, w_m2_)
    S.op("gpsimd", lambda g: g.affine_select(
        out=ident32[:, :], in_=ident32[:, :], compare_op=ALU.not_equal, fill=1.0,
        base=0, pattern=[[-1, 128]], channel_multiplier=1))
    w_msk = S.op("gpsimd", lambda g: g.memset(ones64[:, :], 1.0), pq)

    S.wait("scalar", cq, w_acol)
    w_expA = S.op("scalar", lambda e: e.activation(A_col[:, :], A_col[:, :], AF.Exp), ssc)
    S.wait("vector", ssc, w_expA)
    S.op("vector", lambda e: e.tensor_single_scalar(A_col[:, :], A_col[:, :], -1.0, ALU.mult), vv)
    S.emit("p0")

    with nc.sbuf_tensor("bias_ld", [64, 128], F32) as bias_ld:
        w_b0a = S.dma("sync", cq, bias_ld[:, :], b0[:].rearrange("(a b) -> a b", b=128))
        S.wait("tensor", cq, w_b0a)
        w_t0 = S.op("tensor", lambda e: e.transpose(
            banks[1][0:128, 0:64], bias_ld[:, :], ident32[0:64, 0:64]), mm)
        S.wait("vector", mm, w_t0)
        w_c0 = S.op("vector", lambda e: e.tensor_copy(b0_cols[:, :], banks[1][0:128, 0:64]), vv)
        S.wait("sync", vv, w_c0)
        w_b1a = S.dma("sync", cq, bias_ld[0:16, :], b1[:].rearrange("(a b) -> a b", b=128))
        S.wait("tensor", cq, w_b1a)
        S.wait("tensor", vv, w_c0)
        w_t1 = S.op("tensor", lambda e: e.transpose(
            banks[1][0:128, 0:16], bias_ld[0:16, :], ident32[0:16, 0:16]), mm)
        S.wait("vector", mm, w_t1)
        w_c1 = S.op("vector", lambda e: e.tensor_copy(b1_cols[:, :], banks[1][0:128, 0:16]), vv)
        S.wait("sync", vv, w_c1)
        w_cba = S.dma("sync", cq, bias_ld[0:NXBC, :], conv_b[:].rearrange("(a b) -> a b", b=128))
        S.wait("tensor", cq, w_cba)
        S.wait("tensor", vv, w_c1)
        w_t2 = S.op("tensor", lambda e: e.transpose(
            banks[1][0:128, 0:NXBC], bias_ld[0:NXBC, :], ident32[0:NXBC, 0:NXBC]), mm)
        S.wait("vector", mm, w_t2)
        S.op("vector", lambda e: e.tensor_copy(cb_cols[:, :], banks[1][0:128, 0:NXBC]), vv)
        S.emit("p0bias")



    pB1 = ExitStack()
    BC_fm = pB1.enter_context(nc.sbuf_tensor("BC_fm", [128, 2 * T], F16))
    dAc_tm = pB1.enter_context(nc.sbuf_tensor("dAc_tm", [128, NCHUNK * NH], F32))
    tmq_tm = pB1.enter_context(nc.sbuf_tensor("tmq_tm", [128, NCHUNK * 4 * NH], F16))
    decay_reps = pB1.enter_context(nc.sbuf_tensor("decay_reps", [128, NCHUNK * NH], F32))
    dt_raw = pB1.enter_context(nc.sbuf_tensor("dt_raw", [NH, T], F32))
    pdt2 = ExitStack()
    r_t = pdt2.enter_context(nc.sbuf_tensor("r_t", [NH, T], F32))
    sA = pdt2.enter_context(nc.sbuf_tensor("sA", [NH, T], F32))
    sB = pdt2.enter_context(nc.sbuf_tensor("sB", [NH, T], F32))
    dAc_l = pdt2.enter_context(nc.sbuf_tensor("dAc_l", [NH, T], F32))
    e16 = [pdt2.enter_context(nc.sbuf_tensor(f"e16_{i}", [NH, T], F16)) for i in range(4)]
    ds16 = pdt2.enter_context(nc.sbuf_tensor("ds16", [NH, T], F16))
    dargs = pdt2.enter_context(nc.sbuf_tensor("dargs", [128, NCHUNK * NH], F16))

    pA = ExitStack()
    xnorm_fm = pA.enter_context(nc.sbuf_tensor("xnorm_fm", [128, KD * T], F16))
    if BUILD_UPTO >= 1:
        p1 = ExitStack()
        xt = [p1.enter_context(nc.sbuf_tensor(f"xt{i}", [128, DM], F32)) for i in range(2)]
        xn_t = [p1.enter_context(nc.sbuf_tensor(f"xn_t{i}", [128, DM], F16)) for i in range(2)]
        sq_scr = p1.enter_context(nc.sbuf_tensor("sq_scr", [128, DM], F16))
        cl = p1.enter_context(nc.sbuf_tensor("cl", [128, 8], F32))

        store_done = [0] * 8
        for i in range(8):
            if i >= 2:
                S.wait("sync", gst[i % 2], store_done[i - 2])
            w_ld = S.dma("sync", xq[i % 2], xt[i % 2][:, :], x_in[128 * i:128 * (i + 1), :])
            S.wait("vector", xq[i % 2], w_ld)
            w_rs = S.op("vector", _f(lambda e, i=i: e.reduce_sum(cl[:, 0:1], xt[i % 2][:, :], axis=AX.X)), vv)
            S.wait("vector", vv, w_rs)
            S.op("vector", _f(lambda e: e.tensor_single_scalar(cl[:, 2:3], cl[:, 0:1], 1.0 / DM, ALU.mult)), vv)
            S.wait("scalar", xq[i % 2], w_ld)
            w_sq = S.op("scalar", _f(lambda e, i=i: e.activation(
                sq_scr[:, :], xt[i % 2][:, :], AF.Square, accum_out=cl[:, 1:2])), ssc)
            S.wait("vector", ssc, w_sq)
            S.op("vector", _f(lambda e: e.tensor_single_scalar(cl[:, 3:4], cl[:, 1:2], 1.0 / DM, ALU.mult)))
            w_m2 = S.op("vector", _f(lambda e: e.tensor_tensor(cl[:, 4:5], cl[:, 2:3], cl[:, 2:3], ALU.mult)), vv)
            S.wait("vector", vv, w_m2)
            w_var = S.op("vector", _f(lambda e: e.tensor_tensor(cl[:, 4:5], cl[:, 3:4], cl[:, 4:5], ALU.subtract)), vv)
            S.wait("scalar", vv, w_var)
            w_sqt = S.op("scalar", _f(lambda e: e.activation(cl[:, 5:6], cl[:, 4:5], AF.Sqrt, bias=eps_col[:, 0:1])), ssc)
            S.wait("vector", ssc, w_sqt)
            w_rc = S.op("vector", _f(lambda e: e.reciprocal(cl[:, 6:7], cl[:, 5:6])), vv)
            S.wait("vector", vv, w_rc)
            w_n = S.op("vector", _f(lambda e, i=i: e.tensor_scalar(
                xn_t[i % 2][:, :], xt[i % 2][:, :], cl[:, 2:3], cl[:, 6:7],
                ALU.subtract, ALU.mult)), vv)
            S.wait("sync", vv, w_n)
            store_done[i] = S.dma("sync", gst[i % 2], xn_st[128 * i:128 * (i + 1), :], xn_t[i % 2][:, :])
        S.wait("sync", gst[0], store_done[6])
        S.wait("sync", gst[1], store_done[7])
        if "t_dbg" in DEBUG_TAPS:
            dbg_xt = nc.dram_tensor("dbg_xt", [128, DM], F32, kind="ExternalOutput")
            dbg_xn = nc.dram_tensor("dbg_xn", [128, DM], F32, kind="ExternalOutput")
            dbg_cl = nc.dram_tensor("dbg_cl", [128, 8], F32, kind="ExternalOutput")
            S.dma("sync", st, dbg_xt[:, :], xt[1][:, :])
            S.dma("sync", st, dbg_cl[:, :], cl[:, :])
            w_cv = S.op("vector", lambda e: e.tensor_copy(sq_scr[:, :], xn_t[1][:, :]), vv)
            S.wait("vector", ssc, ssc.v)
            S.wait("sync", vv, w_cv)
            S.op("vector", lambda e: e.tensor_copy(xt[0][:, :], sq_scr[:, :]), vv)
            S.wait("sync", vv, vv.v)
            S.dma("sync", st, dbg_xn[:, :], xt[0][:, :])
            S.wait("sync", st, st.v)
        for k in range(KD):
            S.dma("sync", sq, xnorm_fm[:, T * k:T * (k + 1)],
                  xn_st[:, 128 * k:128 * (k + 1)], transpose=True)
        S.emit("p1")
        p1.close()

    if BUILD_UPTO >= 2:
        p2 = ExitStack()
        wbuf = p2.enter_context(nc.sbuf_tensor("wbuf", [128, 2 * KD * 512], F16))
        pool = [p2.enter_context(nc.sbuf_tensor(f"pool{i}", [128, 1027], F16)) for i in range(3)]
        cacc = [p2.enter_context(nc.sbuf_tensor(f"cacc{i}", [128, 1021], F32)) for i in range(2)]
        convout = [p2.enter_context(nc.sbuf_tensor(f"convout{i}", [128, 1021], F32)) for i in range(2)]
        astg = [p2.enter_context(nc.sbuf_tensor(f"astg{i}", [128, 1021], F16)) for i in range(2)]
        gstg = [p2.enter_context(nc.sbuf_tensor(f"gstg{i}", [128, T], F16)) for i in range(2)]

        blocks = ([(4096 + 512 * i, 512, "xbc") for i in range(8)]
                  + [(8192, 256, "xbc"), (8448, 64, "dt")]
                  + [(512 * i, 512, "gate") for i in range(8)])

        HALF = KD * 512

        pair_evac = []
        slab_used = []
        conv_done = {}
        silu_done = {}
        act_store = {}
        gate_store = {}
        pair_idx = 0
        xbc_fc = 0
        gate_gc = 0
        mm_slab_consumed_prev = 0

        for bi, (f0, w, kind) in enumerate(blocks):
            half = bi % 2
            if bi >= 2:
                S.wait("gpsimd", mm, slab_used[bi - 2])
            for k0 in (0, 8):
                srcap = bass.AP(w_in_T, k0 * 128 * PROJ + f0,
                                [[PROJ, 128], [PROJ * 128, 8], [1, w]])
                dst = bass.AP(wbuf, half * HALF + k0 * w,
                              [[2 * HALF, 128], [w, 8], [1, w]])
                w_slab = S.dma("gpsimd", gqh[half], dst, srcap)
            nfc = max(1, w // 128)
            S.wait("tensor", gqh[half], w_slab)
            if bi == 0:
                S.wait("tensor", sq, sq.v)
            for fc in range(nfc):
                fw = min(128, w)
                fcpar = (pair_idx // 2) % 2
                for tt in range(2):
                    if pair_idx >= 4:
                        c_, v_ = pair_evac[pair_idx - 4]
                        S.wait("tensor", c_, v_)
                    pair_idx += 1
                pair_idx -= 2
                for k in range(KD):
                    for tt in range(2):
                        bk = banks[2 * fcpar + tt]
                        def mmfn(e, bk=bk, half=half, k=k, w=w, fc=fc, tt=tt, fw=fw):
                            return e.matmul(
                                bk[0:fw, :],
                                bass.AP(wbuf, half * HALF + k * w + fc * 128,
                                        [[2 * HALF, 128], [1, fw]]),
                                xnorm_fm[:, T * k + 512 * tt: T * k + 512 * tt + 512],
                                start=(k == 0), stop=(k == KD - 1))
                        if k == KD - 1:
                            w_mm = S.op("tensor", mmfn, mm)
                        else:
                            S.op("tensor", mmfn)
                for tt in range(2):
                    bk = banks[2 * fcpar + tt]
                    if kind == "gate":
                        gc = gate_gc
                        if gc >= 2 and tt == 0:
                            S.wait("scalar", gst[gc % 2], gate_store.get(gc - 2, 0))
                        S.wait("scalar", mm, w_mm)
                        ev = S.op("scalar", _f(lambda e, bk=bk, gc=gc, tt=tt: e.activation(
                            gstg[gc % 2][:, 512 * tt:512 * (tt + 1)], bk[:, :], AF.Copy)), ssc)
                        pair_evac.append((ssc, ev))
                        if tt == 1:
                            S.wait("sync", ssc, ev)
                            gate_store[gc] = S.dma(
                                "sync", gst[gc % 2], gate_st[128 * gc:128 * (gc + 1), :], gstg[gc % 2][:, :])
                            gate_gc += 1
                    elif kind == "dt":
                        S.wait("vector", mm, w_mm)
                        ev = S.op("vector", _f(lambda e, bk=bk, tt=tt: e.tensor_copy(
                            dt_raw[:, 512 * tt:512 * (tt + 1)], bk[0:64, :])), vv)
                        pair_evac.append((vv, ev))
                    else:
                        fc_g = xbc_fc
                        if fc_g >= 3 and tt == 0:
                            S.wait("scalar", vv, conv_done.get(fc_g - 3, 0))
                        S.wait("scalar", mm, w_mm)
                        ncols = 512 if fw == 128 else 512
                        ev = S.op("scalar", _f(lambda e, bk=bk, fc_g=fc_g, tt=tt, fw=fw: e.activation(
                            pool[fc_g % 3][0:fw, 3 + 512 * tt:3 + 512 * (tt + 1)],
                            bk[0:fw, :], AF.Copy)), ssc)
                        pair_evac.append((ssc, ev))
                        if tt == 1:
                            S.wait("vector", ssc, ev)
                            if fc_g >= 2:
                                S.wait("vector", ssc, silu_done.get(fc_g - 2, 0))
                            pt = pool[fc_g % 3]
                            co = convout[fc_g % 2]
                            S.op("vector", _f(lambda e, pt=pt, fc_g=fc_g: e.tensor_scalar(
                                cacc[0][:, :], pt[:, 3:1024], cw_cols[:, 4 * fc_g:4 * fc_g + 1],
                                cb_cols[:, fc_g:fc_g + 1], ALU.mult, ALU.add)))
                            S.op("vector", _f(lambda e, pt=pt, fc_g=fc_g: e.scalar_tensor_tensor(
                                cacc[1][:, :], pt[:, 4:1025], cw_cols[:, 4 * fc_g + 1:4 * fc_g + 2],
                                cacc[0][:, :], ALU.mult, ALU.add)))
                            S.op("vector", _f(lambda e, pt=pt, fc_g=fc_g: e.scalar_tensor_tensor(
                                cacc[0][:, :], pt[:, 5:1026], cw_cols[:, 4 * fc_g + 2:4 * fc_g + 3],
                                cacc[1][:, :], ALU.mult, ALU.add)))
                            S.op("vector", _f(lambda e, pt=pt, co=co, fc_g=fc_g: e.scalar_tensor_tensor(
                                co[:, :], pt[:, 6:1027], cw_cols[:, 4 * fc_g + 3:4 * fc_g + 4],
                                cacc[0][:, :], ALU.mult, ALU.add)))
                            S.op("vector", _f(lambda e, pt=pt, fc_g=fc_g: e.tensor_copy(
                                stash_first[:, 3 * fc_g:3 * fc_g + 3], pt[:, 3:6])))
                            cd = S.op("vector", _f(lambda e, pt=pt, fc_g=fc_g: e.tensor_copy(
                                stash_halo[:, 3 * fc_g:3 * fc_g + 3], pt[:, 1024:1027])), vv)
                            conv_done[fc_g] = cd
                            S.wait("scalar", vv, cd)
                            if fc_g >= 2:
                                S.wait("scalar", ast[fc_g % 2], act_store.get(fc_g - 2, 0))
                            sd = S.op("scalar", _f(lambda e, co=co, fc_g=fc_g: e.activation(
                                astg[fc_g % 2][:, :], co[:, :], AF.Silu)), ssc)
                            silu_done[fc_g] = sd
                            if fc_g in (32, 33):
                                S.wait("vector", ssc, sd)
                                S.op("vector", _f(lambda e, fc_g=fc_g: e.tensor_copy(
                                    BC_fm[:, (fc_g - 32) * T + 3:(fc_g - 32) * T + 1024],
                                    astg[fc_g % 2][:, :])), vv)
                            S.wait("sync", ssc, sd)
                            act_store[fc_g] = S.dma(
                                "sync", ast[fc_g % 2],
                                bass.AP(act_st, fc_g * 128 * T + 3, [[T, 128], [1, 1021]]),
                                astg[fc_g % 2][:, :])
                            xbc_fc += 1
                    pair_idx += 1
            slab_used.append(mm.v)
        if "t_gate" in taps:
            with nc.sbuf_tensor("gtap", [128, T], F32) as gtap:
                for gc in range(32):
                    S.wait("gpsimd", st, st.v)
                    w_l = S.dma("gpsimd", gq, gtap[:, :], gate_st[128 * gc:128 * (gc + 1), :])
                    S.wait("sync", gq, w_l)
                    S.dma("sync", st, taps["t_gate"][128 * gc:128 * (gc + 1), :], gtap[:, :])
                S.wait("sync", st, st.v)
                S.emit("tapg")

    if BUILD_UPTO >= 3:

        S.wait("sync", sq, sq.v)
        S.wait("vector", sq, sq.v)
        S.wait("gpsimd", sq, sq.v)
        S.wait("scalar", sq, sq.v)
        S.wait("tensor", sq, sq.v)
        w_r = S.op("vector", lambda e: e.tensor_scalar_add(r_t[:, :], dt_raw[:, :], dtb_col[:, 0:1]), vv)
        S.wait("scalar", vv, w_r)
        S.op("scalar", lambda e: e.activation(sA[:, :], r_t[:, :], AF.Abs))
        S.op("scalar", lambda e: e.activation(sB[:, :], sA[:, :], AF.Exp, scale=-1.0))
        w_ln = S.op("scalar", lambda e: e.activation(sA[:, :], sB[:, :], AF.Ln, bias=1.0), ssc)
        S.wait("vector", ssc, w_ln)
        S.op("vector", lambda e: e.tensor_scalar_max(sB[:, :], r_t[:, :], 0.0))
        S.op("vector", lambda e: e.tensor_add(r_t[:, :], sA[:, :], sB[:, :]))
        S.op("vector", lambda e: e.tensor_copy(e16[0][:, :], r_t[:, :]))
        S.op("vector", lambda e: e.tensor_scalar_mul(sA[:, :], r_t[:, :], A_col[:, 0:1]))
        for c in range(NCHUNK):
            S.op("vector", _f(lambda e, c=c: e.tensor_tensor_scan(
                dAc_l[:, CH * c:CH * (c + 1)], ones64[:, 0:CH], sA[:, CH * c:CH * (c + 1)],
                0.0, ALU.mult, ALU.add)))
        w_scan = S.op("vector", lambda e: e.tensor_tensor_scan(
            sB[:, :], ones64[:, :], sA[:, :], 0.0, ALU.mult, ALU.add), vv)
        S.wait("scalar", vv, w_scan)
        S.op("scalar", lambda e: e.activation(e16[2][:, :], dAc_l[:, :], AF.Exp))
        w_eg = S.op("scalar", lambda e: e.activation(e16[3][:, :], sB[:, :], AF.Exp), ssc)
        for c in range(NCHUNK):
            S.op("scalar", _f(lambda e, c=c: e.activation(
                ds16[:, CH * c:CH * (c + 1)], dAc_l[:, CH * c:CH * (c + 1)], AF.Exp,
                scale=-1.0, bias=dAc_l[:, CH * c + CH - 1:CH * c + CH])))
        w_ds = S.op("scalar", lambda e: e.activation(ds16[:, 0:1], ds16[:, 0:1], AF.Copy), ssc)
        S.wait("vector", ssc, w_ds)
        w_fac = S.op("vector", lambda e: e.tensor_mul(e16[1][:, :], r_t[:, :], ds16[:, :]), vv)

        S.wait("sync", vv, w_fac)
        S.dma("sync", st, dAc_st[:, :], dAc_l[:, :])
        for qd in range(4):
            S.dma("sync", st, tmq_st[NH * qd:NH * (qd + 1), :], e16[qd][:, :])
        S.wait("gpsimd", vv, w_scan)
        for c in range(NCHUNK):
            w_das = S.dma("gpsimd", gq, dasum_st[c:c + 1, :],
                          bass.AP(dAc_l, CH * c + CH - 1, [[T, NH], [1, 1]]))
        S.wait("sync", gq, w_das)
        w_darg = S.dma("sync", sq, dargs[:, :],
                       bass.AP(dasum_st, 0, [[0, 128], [1, NCHUNK * NH]]))
        S.wait("scalar", sq, w_darg)
        S.op("scalar", lambda e: e.activation(decay_reps[:, :], dargs[:, :], AF.Exp), ssc)
        S.wait("sync", st, st.v)
        for c in range(NCHUNK):
            S.dma("sync", sq, tmq_tm[:, 256 * c:256 * (c + 1)],
                  tmq_st[:, CH * c:CH * (c + 1)], transpose=True)
        S.wait("tensor", vv, w_scan)
        for c in range(NCHUNK):
            w_tr = S.op("tensor", _f(lambda e, c=c: e.transpose(
                banks[2][0:128, 0:64], dAc_l[:, CH * c:CH * (c + 1)],
                ident32[0:64, 0:64])), mm)
            S.wait("vector", mm, w_tr)
            w_cp = S.op("vector", _f(lambda e, c=c: e.tensor_copy(
                dAc_tm[:, NH * c:NH * (c + 1)], banks[2][0:128, 0:64])), vv)
            S.wait("tensor", vv, w_cp)
        if "t_dt" in taps:
            S.dma("sync", st, taps["t_dt"][:, :], r_t[:, :])
        if "t_dAc" in taps:
            S.dma("sync", st, taps["t_dAc"][:, :], dAc_l[:, :])
        S.wait("sync", st, st.v)
        S.wait("sync", sq, sq.v)

    if BUILD_UPTO >= 4:
        w_send = S.op("vector", lambda e: e.tensor_scalar_mul(
            stash_halo[:, :], stash_halo[:, :], flagA[:, 0:1]), vv)
        S.wait("gpsimd", vv, w_send)
        w_ci = S.dma("gpsimd", gq,
                     bass.AP(cc1_in, 0, [[3, 128], [3 * 128, NXBC], [1, 3]]),
                     bass.AP(stash_halo, 0, [[NXBC * 3, 128], [3, NXBC], [1, 3]]))
        S.wait("gpsimd", gq, w_ci)
        ccs.v += 1
        S.q["gpsimd"].append(lambda g: g.collective_compute(
            "AllReduce", ALU.add, replica_groups=[[0, 1], [2, 3], [4, 5], [6, 7]],
            ins=[cc1_in.ap().opt()], outs=[cc1_out.ap().opt()]).then_inc(ccs.h, 1))
        S.wait("gpsimd", ccs, ccs.v)
        w_co = S.dma("gpsimd", gq,
                     bass.AP(stash_halo, 0, [[NXBC * 3, 128], [3, NXBC], [1, 3]]),
                     bass.AP(cc1_out, 0, [[3, 128], [3 * 128, NXBC], [1, 3]]))
        S.wait("vector", gq, w_co)
        S.op("vector", lambda e: e.tensor_scalar_mul(
            stash_halo[:, :], stash_halo[:, :], flagB[:, 0:1]))
        p3s = ExitStack()
        six_all = p3s.enter_context(nc.sbuf_tensor("six_all", [128, NXBC * 6], F32))
        c3a = p3s.enter_context(nc.sbuf_tensor("c3a", [128, NXBC * 3], F32))
        S.op("vector", lambda e: e.tensor_copy(
            six_all[:, :].rearrange("p (f c) -> p f c", c=6)[:, :, 0:3],
            stash_halo[:, :].rearrange("p (f c) -> p f c", c=3)))
        w_pk = S.op("vector", lambda e: e.tensor_copy(
            six_all[:, :].rearrange("p (f c) -> p f c", c=6)[:, :, 3:6],
            stash_first[:, :].rearrange("p (f c) -> p f c", c=3)), vv)
        S.wait("vector", vv, w_pk)

        def _win(i):
            return six_all[:, :].rearrange("p (f c) -> p f c", c=6)[:, :, i:i + 3]

        def _cw(i):
            return bass.AP(cw_cols, i, [[NXBC * 4, 128], [4, NXBC], [0, 3]])

        cb_b = bass.AP(cb_cols, 0, [[NXBC, 128], [1, NXBC], [0, 3]])
        w_t0 = S.op("vector", lambda e: e.tensor_tensor(
            fixpack[:, :].rearrange("p (f c) -> p f c", c=3), _win(0), _cw(0), ALU.mult), vv)
        S.wait("vector", vv, w_t0)
        w_t0b = S.op("vector", lambda e: e.tensor_tensor(
            fixpack[:, :].rearrange("p (f c) -> p f c", c=3),
            fixpack[:, :].rearrange("p (f c) -> p f c", c=3), cb_b, ALU.add), vv)
        S.wait("vector", vv, w_t0b)
        for i in (1, 2, 3):
            w_ta = S.op("vector", _f(lambda e, i=i: e.tensor_tensor(
                c3a[:, :].rearrange("p (f c) -> p f c", c=3), _win(i), _cw(i), ALU.mult)), vv)
            S.wait("vector", vv, w_ta)
            w_tb = S.op("vector", _f(lambda e: e.tensor_add(
                fixpack[:, :], fixpack[:, :], c3a[:, :])), vv)
            S.wait("vector", vv, w_tb)
        S.wait("scalar", vv, vv.v)
        w_silu3 = S.op("scalar", lambda e: e.activation(fixp16[:, :], fixpack[:, :], AF.Silu), ssc)
        S.wait("sync", ssc, w_silu3)
        S.dma("sync", st, bass.AP(act_st, 0, [[T, 128], [128 * T, NXBC], [1, 3]]),
              bass.AP(fixp16, 0, [[NXBC * 3, 128], [3, NXBC], [1, 3]]))
        S.wait("vector", ssc, w_silu3)
        S.op("vector", lambda e: e.tensor_copy(BC_fm[:, 0:3], fixp16[:, 96:99]))
        S.op("vector", lambda e: e.tensor_copy(BC_fm[:, T:T + 3], fixp16[:, 99:102]), ctr=None)
        S.wait("sync", st, st.v)
        S.emit("p2all")
        p3s.close()
        p2.close()
        pA.close()
        pdt2.close()

        if "t_act" in taps:
            p3t = ExitStack()
            atap = p3t.enter_context(nc.sbuf_tensor("atap", [128, T], F32))
            for fc in range(NXBC):
                S.wait("gpsimd", st, st.v)
                w_l = S.dma("gpsimd", gq, atap[:, :], act_st[128 * fc:128 * (fc + 1), :])
                S.wait("sync", gq, w_l)
                S.dma("sync", st, taps["t_act"][128 * fc:128 * (fc + 1), :], atap[:, :])
            S.wait("sync", st, st.v)
            S.emit("tapa")
            p3t.close()

    pB2 = ExitStack()
    y_tm = pB2.enter_context(nc.sbuf_tensor("y_tm", [128, NCHUNK * DIN], F16))
    S_f32 = pB2.enter_context(nc.sbuf_tensor("S_f32", [128, DIN], F32))
    S_f16 = pB2.enter_context(nc.sbuf_tensor("S_f16", [128, DIN], F16))
    SSt = pB2.enter_context(nc.sbuf_tensor("SSt", [128, DIN], F16))
    nw_rep = pB2.enter_context(nc.sbuf_tensor("nw_rep", [128, DIN], F16))

    if BUILD_UPTO >= 5:
        p4 = ExitStack()
        xb = [p4.enter_context(nc.sbuf_tensor(f"xb{i}", [128, CONVD], F16)) for i in range(2)]
        rq = [p4.enter_context(nc.sbuf_tensor(f"rq{i}", [128, 16 * CH], F32)) for i in range(2)]
        larg = p4.enter_context(nc.sbuf_tensor("larg", [128, 16 * CH], F32))
        me16 = p4.enter_context(nc.sbuf_tensor("me16", [128, 16 * CH], F16))
        mh16 = [p4.enter_context(nc.sbuf_tensor(f"mh16{i}", [128, 16 * CH], F16)) for i in range(4)]
        gt_sb = [p4.enter_context(nc.sbuf_tensor(f"gt_sb{i}", [128, 128], F16)) for i in range(2)]
        xsc = [p4.enter_context(nc.sbuf_tensor(f"xsc{i}", [128, 512], F16)) for i in range(2)]
        xst_ = [p4.enter_context(nc.sbuf_tensor(f"xst{i}", [128, 512], F16)) for i in range(2)]
        tmpv = [p4.enter_context(nc.sbuf_tensor(f"tmpv{i}", [128, 512], F32)) for i in range(2)]
        skp = [p4.enter_context(nc.sbuf_tensor(f"skp{i}", [128, 512], F16)) for i in range(2)]

        S.wait("sync", ast[0], ast[0].v)
        S.wait("sync", ast[1], ast[1].v)
        S.wait("sync", st, st.v)
        S.op("gpsimd", lambda e: e.memset(S_f32[:, :], 0.0))
        w_s0 = S.op("gpsimd", lambda e: e.memset(S_f16[:, :], 0.0), pq)
        S.wait("tensor", pq, w_s0)
        S.wait("vector", pq, w_s0)

        xb_ready = [0] * NCHUNK
        rq_ready = {}
        xb_freed = {}
        rq_freed = {}
        rq_freed_ctr = {}
        mh_pool = {}
        mh_w = {}
        mh_freed = {}
        xsc_freed = {}
        ev_freed = {}

        xb_ready[0] = S.dma("sync", xq[0], xb[0][:, :], act_st[:, 0:CH], transpose=True)
        rq_ready[(0, 0)] = S.dma("sync", rqs[0], rq[0][:, :],
                                 bass.AP(dAc_st, 0, [[0, 128], [T, 16], [1, CH]]))

        yd_bank = {0: banks[2], 1: banks[4]}
        yo_bank = {0: banks[3], 1: banks[5]}
        st_bank = {0: banks[6], 1: banks[7]}

        last_dve_group = {}
        for c in range(NCHUNK):
            S.wait("tensor", xq[c % 2], xb_ready[c])
            if c >= 1:
                S.wait("tensor", vv, gt_evac)
            if c + 1 < NCHUNK:
                if c >= 1:
                    S.wait("sync", mm, xb_freed.get(c - 1, 0))
                xb_ready[c + 1] = S.dma("sync", xq[(c + 1) % 2], xb[(c + 1) % 2][:, :],
                                        act_st[:, CH * (c + 1):CH * (c + 2)], transpose=True)
            w_gt = S.op("tensor", _f(lambda e, c=c: e.matmul(
                banks[0][0:128, 0:128], BC_fm[:, CH * c:CH * (c + 1)],
                BC_fm[:, T + CH * c:T + CH * (c + 1)], start=True, stop=True)), mm)
            S.wait("vector", mm, w_gt)
            gt_evac = S.op("vector", _f(lambda e, c=c: e.tensor_tensor(
                gt_sb[c % 2][:, :], banks[0][0:128, 0:128], tril01[:, :], ALU.mult)), vv)

            for q in range(4):
                S.wait("vector", rqs[(c * 4 + q) % 2], rq_ready[(c, q)])

                nqc, nqq = (c, q + 1) if q < 3 else (c + 1, 0)
                if nqc < NCHUNK:
                    pos = nqc * 4 + nqq
                    if pos >= 2:
                        ppc, ppq = divmod(pos - 2, 4)
                        S.wait("sync", rq_freed_ctr.get((ppc, ppq), vv),
                               rq_freed.get((ppc, ppq), 0))
                    rq_ready[(nqc, nqq)] = S.dma(
                        "sync", rqs[pos % 2], rq[pos % 2][:, :],
                        bass.AP(dAc_st, 16 * T * nqq + CH * nqc,
                                [[0, 128], [T, 16], [1, CH]]))
                rbuf = rq[(c * 4 + q) % 2]
                w_lg = S.op("vector", _f(lambda e, c=c, q=q, rbuf=rbuf: e.tensor_tensor(
                    larg[:, :].rearrange("p (h t) -> p h t", h=16),
                    rbuf[:, :].rearrange("p (h t) -> p h t", h=16),
                    _bcast_hp(dAc_tm[:, NH * c + 16 * q:NH * c + 16 * (q + 1)], 16, CH),
                    ALU.subtract)), vv)
                S.wait("vector", vv, w_lg)
                w_cl = S.op("vector", _f(lambda e, rbuf=rbuf: e.tensor_scalar_min(
                    rbuf[:, :], larg[:, :], 0.0)), vv)
                rq_freed[(c, q)] = w_cl
                rq_freed_ctr[(c, q)] = vv
                S.wait("scalar", vv, w_cl)
                if c * 4 + q >= 1:
                    mpc, mpq = divmod(c * 4 + q - 1, 4)
                    S.wait("scalar", vv, mh_pool.get((mpc, mpq), 0))
                w_me = S.op("scalar", _f(lambda e, rbuf=rbuf: e.activation(
                    me16[:, :], rbuf[:, :], AF.Exp)), ssc)
                S.wait("vector", ssc, w_me)
                S.wait("vector", vv, gt_evac)
                if c >= 1:
                    S.wait("vector", mm, mh_freed.get((c - 1, q), 0))
                w_mh = S.op("vector", _f(lambda e, c=c, q=q: e.tensor_tensor(
                    mh16[q][:, :].rearrange("p (h t) -> p h t", h=16),
                    me16[:, :].rearrange("p (h t) -> p h t", h=16),
                    _bcast_outer(gt_sb[c % 2][:, :], 16), ALU.mult)), vv)
                mh_pool[(c, q)] = w_mh
                mh_w[(c, q)] = w_mh

            if True:
                for g in range(8):
                    q = g // 2
                    w_mh = mh_w[(c, q)]
                    par = g % 2
                    if c * 8 + g >= 2:
                        S.wait("vector", mm, xsc_freed.get((c * 8 + g - 2), 0))
                    S.op("vector", _f(lambda e, c=c, g=g, par=par: e.tensor_tensor(
                        xsc[par][:, :].rearrange("p (h d) -> p h d", h=8),
                        xb[c % 2][:, 512 * g:512 * (g + 1)].rearrange("p (h d) -> p h d", h=8),
                        _bcast_hp(tmq_tm[:, 256 * c + 8 * g:256 * c + 8 * (g + 1)], 8, HPD),
                        ALU.mult)))
                    w_xs = S.op("vector", _f(lambda e, c=c, g=g, par=par: e.tensor_tensor(
                        xst_[par][:, :].rearrange("p (h d) -> p h d", h=8),
                        xb[c % 2][:, 512 * g:512 * (g + 1)].rearrange("p (h d) -> p h d", h=8),
                        _bcast_hp(tmq_tm[:, 256 * c + 64 + 8 * g:256 * c + 64 + 8 * (g + 1)], 8, HPD),
                        ALU.mult)), vv)
                    S.wait("tensor", vv, w_mh)
                    S.wait("tensor", vv, w_xs)
                    S.wait("tensor", vv, ev_freed.get(("yd", par, c, g), 0))
                    for hl in range(8):
                        h = 8 * g + hl
                        hq = h - 16 * q
                        def ydfn(e, c=c, q=q, par=par, hl=hl, hq=hq, h=h):
                            return e.matmul(
                                yd_bank[par][0:128, 64 * hl:64 * (hl + 1)],
                                mh16[q][:, 128 * hq:128 * (hq + 1)],
                                xsc[par][:, 64 * hl:64 * (hl + 1)],
                                start=(hl == 0), stop=(hl == 7))
                        if hl == 7:
                            w_yd = S.op("tensor", ydfn, mm)
                        else:
                            S.op("tensor", ydfn)
                    S.wait("tensor", vv, ev_freed.get(("yo", par, c, g), 0))
                    w_yo = S.op("tensor", _f(lambda e, c=c, g=g, par=par: e.matmul(
                        yo_bank[par][0:128, :], BC_fm[:, T + CH * c:T + CH * (c + 1)],
                        S_f16[:, 512 * g:512 * (g + 1)], start=True, stop=True)), mm)
                    S.wait("tensor", vv, ev_freed.get(("st", par, c, g), 0))
                    w_st = S.op("tensor", _f(lambda e, c=c, g=g, par=par: e.matmul(
                        st_bank[par][0:128, :], xb[c % 2][:, DIN:DIN + 128],
                        xst_[par][:, :], start=True, stop=True)), mm)
                    if g % 2 == 1:
                        mh_freed[(c, q)] = w_yd
                    xsc_freed[c * 8 + g] = w_st
                    if g == 7:
                        xb_freed[c] = w_st

                    S.wait("vector", mm, w_yo)
                    S.op("vector", _f(lambda e, c=c, g=g, par=par: e.tensor_tensor(
                        tmpv[par][:, :].rearrange("p (h d) -> p h d", h=8),
                        yo_bank[par][0:128, :].rearrange("p (h d) -> p h d", h=8),
                        _bcast_hp(tmq_tm[:, 256 * c + 128 + 8 * g:256 * c + 128 + 8 * (g + 1)], 8, HPD),
                        ALU.mult)))
                    S.wait("vector", mm, w_yd)
                    ev1 = S.op("vector", _f(lambda e, c=c, g=g, par=par: e.tensor_tensor(
                        y_tm[:, DIN * c + 512 * g:DIN * c + 512 * (g + 1)],
                        tmpv[par][:, :], yd_bank[par][0:128, :], ALU.add)), vv)
                    ev_freed[("yd", par, c + (1 if g >= 6 else 0), (g + 2) % 8)] = ev1
                    ev_freed[("yo", par, c + (1 if g >= 6 else 0), (g + 2) % 8)] = ev1
                    S.op("vector", _f(lambda e, c=c, g=g, par=par: e.tensor_tensor(
                        skp[par][:, :].rearrange("p (h d) -> p h d", h=8),
                        xb[c % 2][:, 512 * g:512 * (g + 1)].rearrange("p (h d) -> p h d", h=8),
                        _bcast_hp(D_rep[:, 8 * g:8 * (g + 1)], 8, HPD), ALU.mult)))
                    S.op("vector", _f(lambda e, c=c, g=g, par=par: e.tensor_tensor(
                        y_tm[:, DIN * c + 512 * g:DIN * c + 512 * (g + 1)],
                        y_tm[:, DIN * c + 512 * g:DIN * c + 512 * (g + 1)],
                        skp[par][:, :], ALU.add)))
                    S.wait("vector", mm, w_st)
                    S.op("vector", _f(lambda e, c=c, g=g, par=par: e.tensor_tensor(
                        tmpv[par][:, :].rearrange("p (h d) -> p h d", h=8),
                        S_f32[:, 512 * g:512 * (g + 1)].rearrange("p (h d) -> p h d", h=8),
                        _bcast_hp(decay_reps[:, NH * c + 8 * g:NH * c + 8 * (g + 1)], 8, HPD),
                        ALU.mult)))
                    S.op("vector", _f(lambda e, c=c, g=g, par=par: e.tensor_add(
                        S_f32[:, 512 * g:512 * (g + 1)], tmpv[par][:, :],
                        st_bank[par][0:128, :])))
                    ev2 = S.op("vector", _f(lambda e, c=c, g=g: e.tensor_copy(
                        S_f16[:, 512 * g:512 * (g + 1)], S_f32[:, 512 * g:512 * (g + 1)])), vv)
                    ev_freed[("st", par, c + (1 if g >= 6 else 0), (g + 2) % 8)] = ev2
                    last_dve_group[(c, g)] = ev2
        S.emit("p4")
        p4.close()

        if "t_S" in taps:
            with nc.sbuf_tensor("stap", [128, DIN], F32) as stap:
                S.op("vector", lambda e: e.tensor_copy(stap[:, :], S_f32[:, :]), vv)
                S.wait("sync", vv, vv.v)
                S.dma("sync", st, taps["t_S"][:, :], stap[:, :])
                S.wait("sync", st, st.v)
                S.emit("taps4")

    if BUILD_UPTO >= 6:
        p5 = ExitStack()
        ar_scr = p5.enter_context(nc.sbuf_tensor("ar_scr", [128, DIN], F32))
        w_snd = S.op("vector", lambda e: e.tensor_scalar_mul(
            ar_scr[:, :], S_f32[:, :], flagA[:, 0:1]), vv)
        S.wait("gpsimd", vv, w_snd)
        w_ci2 = S.dma("gpsimd", gq, cc2_in[:, :], ar_scr[:, :])
        S.wait("gpsimd", gq, w_ci2)
        ccs.v += 1
        S.q["gpsimd"].append(lambda g: g.collective_compute(
            "AllReduce", ALU.add, replica_groups=[[0, 1], [2, 3], [4, 5], [6, 7]],
            ins=[cc2_in.ap().opt()], outs=[cc2_out.ap().opt()]).then_inc(ccs.h, 1))
        S.wait("gpsimd", ccs, ccs.v)
        w_co2 = S.dma("gpsimd", gq, ar_scr[:, :], cc2_out[:, :])
        S.wait("vector", gq, w_co2)
        w_ss = S.op("vector", lambda e: e.tensor_scalar_mul(
            SSt[:, :], ar_scr[:, :], flagB[:, 0:1]), vv)
        S.emit("p5")
        p5.close()

        p6 = ExitStack()
        tmpe = p6.enter_context(nc.sbuf_tensor("tmpe", [128, DIN], F32))
        ep_ev = {}
        for c in range(NCHUNK):
            for g in range(8):
                par = g % 2
                S.wait("tensor", vv, ep_ev.get((c, g), 0))
                w_yx = S.op("tensor", _f(lambda e, c=c, g=g, par=par: e.matmul(
                    yd_bank[par][0:128, :], BC_fm[:, T + CH * c:T + CH * (c + 1)],
                    SSt[:, 512 * g:512 * (g + 1)], start=True, stop=True)), mm)
                S.wait("vector", mm, w_yx)
                w_sc = S.op("vector", _f(lambda e, c=c, g=g, par=par: e.tensor_tensor(
                    tmpe[:, 512 * g:512 * (g + 1)].rearrange("p (h d) -> p h d", h=8),
                    yd_bank[par][0:128, :].rearrange("p (h d) -> p h d", h=8),
                    _bcast_hp(tmq_tm[:, 256 * c + 192 + 8 * g:256 * c + 192 + 8 * (g + 1)], 8, HPD),
                    ALU.mult)), vv)
                ep_ev[(c, g + 2) if g < 6 else (c + 1, (g + 2) % 8)] = w_sc
            w_ad = S.op("vector", _f(lambda e, c=c: e.tensor_add(
                y_tm[:, DIN * c:DIN * (c + 1)], y_tm[:, DIN * c:DIN * (c + 1)],
                tmpe[:, :])), vv)
            w_ssu = S.op("vector", _f(lambda e, c=c: e.tensor_tensor(
                SSt[:, :].rearrange("p (h d) -> p h d", h=NH),
                SSt[:, :].rearrange("p (h d) -> p h d", h=NH),
                _bcast_hp(decay_reps[:, NH * c:NH * (c + 1)], NH, HPD), ALU.mult)), vv)
            S.wait("tensor", vv, w_ssu)
        S.emit("p6")
        p6.close()

        if "t_y" in taps:
            with nc.sbuf_tensor("ytp", [128, DIN], F32) as ytp:
                for c in range(NCHUNK):
                    S.wait("vector", st, st.v)
                    S.op("vector", _f(lambda e, c=c: e.tensor_copy(
                        ytp[:, :], y_tm[:, DIN * c:DIN * (c + 1)])), vv)
                    S.wait("sync", vv, vv.v)
                    S.dma("sync", st, taps["t_y"][CH * c:CH * (c + 1), :], ytp[:, :])
                S.wait("sync", st, st.v)
                S.emit("tapy")

    if BUILD_UPTO >= 7:
        w_nw = S.dma("gpsimd", gq, nw_rep[:, :], bass.AP(norm_w, 0, [[0, 128], [1, DIN]]))
        p7 = ExitStack()
        gtm = [p7.enter_context(nc.sbuf_tensor(f"gtm{i}", [128, DIN], F16)) for i in range(2)]
        sgt = p7.enter_context(nc.sbuf_tensor("sgt", [128, DIN], F16))
        hgt = p7.enter_context(nc.sbuf_tensor("hgt", [128, DIN], F16))
        cl7 = p7.enter_context(nc.sbuf_tensor("cl7", [128, 4], F32))
        S.wait("sync", gst[0], gst[0].v)
        S.wait("sync", gst[1], gst[1].v)
        g_ready = [0] * NCHUNK
        g_ready[0] = S.dma("sync", xq[0], gtm[0][:, :], gate_st[:, 0:CH], transpose=True)
        yst = {}
        S.wait("vector", gq, w_nw)
        sg_hist = {}
        for c in range(NCHUNK):
            S.wait("scalar", xq[c % 2], g_ready[c])
            if c + 1 < NCHUNK:
                if c >= 1:
                    S.wait("sync", ssc, sg_hist.get(c - 1, 0))
                g_ready[c + 1] = S.dma("sync", xq[(c + 1) % 2], gtm[(c + 1) % 2][:, :],
                                       gate_st[:, CH * (c + 1):CH * (c + 2)], transpose=True)
            S.wait("scalar", gst[(c - 1) % 2], yst.get(c - 1, 0))
            w_sg = S.op("scalar", _f(lambda e, c=c: e.activation(
                sgt[:, :], gtm[c % 2][:, :], AF.Silu)), ssc)
            sg_hist[c] = w_sg
            S.wait("vector", ssc, w_sg)
            w_hg = S.op("vector", _f(lambda e, c=c: e.tensor_tensor(
                hgt[:, :], y_tm[:, DIN * c:DIN * (c + 1)], sgt[:, :], ALU.mult)), vv)
            S.wait("scalar", vv, w_hg)
            w_sq2 = S.op("scalar", _f(lambda e: e.activation(
                sgt[:, :], hgt[:, :], AF.Square, accum_out=cl7[:, 0:1])), ssc)
            S.wait("vector", ssc, w_sq2)
            S.op("vector", _f(lambda e: e.tensor_single_scalar(
                cl7[:, 1:2], cl7[:, 0:1], 1.0 / DIN, ALU.mult)), vv)
            S.wait("scalar", vv, vv.v)
            w_sq3 = S.op("scalar", _f(lambda e: e.activation(
                cl7[:, 2:3], cl7[:, 1:2], AF.Sqrt, bias=eps_col[:, 0:1])), ssc)
            S.wait("vector", ssc, w_sq3)
            w_rc7 = S.op("vector", _f(lambda e: e.reciprocal(cl7[:, 3:4], cl7[:, 2:3])), vv)
            S.wait("vector", vv, w_rc7)
            S.op("vector", _f(lambda e: e.tensor_scalar_mul(sgt[:, :], hgt[:, :], cl7[:, 3:4])))
            w_yn = S.op("vector", _f(lambda e: e.tensor_mul(hgt[:, :], sgt[:, :], nw_rep[:, :])), vv)
            S.wait("sync", vv, w_yn)
            yst[c] = S.dma("sync", gst[c % 2], yn_st[CH * c:CH * (c + 1), :], hgt[:, :])
        S.wait("sync", gst[0], gst[0].v)
        S.wait("sync", gst[1], gst[1].v)
        S.emit("p7")
        p7.close()
        emit_dram_tap("t_yn", yn_st, T, DIN)
    pB2.close()
    pB1.close()

    if BUILD_UPTO >= 8:
        p8 = ExitStack()
        ynfm = p8.enter_context(nc.sbuf_tensor("ynfm", [128, 32 * T], F16))
        wo_buf = p8.enter_context(nc.sbuf_tensor("wo_buf", [128, 2 * 32 * 512], F16))
        astg8 = [p8.enter_context(nc.sbuf_tensor(f"astg8_{i}", [128, T], F16)) for i in range(2)]
        for k in range(32):
            w_yl = S.dma("sync", bq, ynfm[:, T * k:T * (k + 1)],
                         yn_st[:, 128 * k:128 * (k + 1)], transpose=True)
        WHALF = 32 * 512
        ev8 = {}
        sto8 = {}
        slab_mm8 = []
        for fb in range(4):
            half = fb % 2
            if fb >= 2:
                S.wait("gpsimd", mm, slab_mm8[fb - 2])
            for k0 in (0, 8, 16, 24):
                w_sl = S.dma("gpsimd", gqh[half],
                             bass.AP(wo_buf, half * WHALF + k0 * 512,
                                     [[2 * WHALF, 128], [512, 8], [1, 512]]),
                             bass.AP(w_out_T, k0 * 128 * DM + 512 * fb,
                                     [[DM, 128], [DM * 128, 8], [1, 512]]))
            S.wait("tensor", gqh[half], w_sl)
            if fb == 0:
                S.wait("tensor", bq, w_yl)
            for fc in range(4):
                f_g = 4 * fb + fc
                for tt in range(2):
                    pi = (f_g * 2 + tt) % 2
                    S.wait("tensor", ssc, ev8.get((f_g * 2 + tt) - 2, 0))
                    for k in range(32):
                        def mo(e, half=half, k=k, fc=fc, tt=tt, pi=pi):
                            return e.matmul(
                                banks[pi][0:128, :],
                                bass.AP(wo_buf, half * WHALF + k * 512 + fc * 128,
                                        [[2 * WHALF, 128], [1, 128]]),
                                ynfm[:, T * k + 512 * tt:T * k + 512 * tt + 512],
                                start=(k == 0), stop=(k == 31))
                        if k == 31:
                            w_m8 = S.op("tensor", mo, mm)
                        else:
                            S.op("tensor", mo)
                    S.wait("scalar", mm, w_m8)
                    if tt == 0:
                        S.wait("scalar", ast[f_g % 2], sto8.get(f_g - 2, 0))
                    ev8[f_g * 2 + tt] = S.op("scalar", _f(
                        lambda e, f_g=f_g, tt=tt, pi=pi: e.activation(
                            astg8[f_g % 2][:, 512 * tt:512 * (tt + 1)],
                            banks[pi][0:128, :], AF.Copy)), ssc)
                    if tt == 1:
                        S.wait("sync", ssc, ev8[f_g * 2 + 1])
                        sto8[f_g] = S.dma("sync", ast[f_g % 2],
                                          attn_st[128 * f_g:128 * (f_g + 1), :],
                                          astg8[f_g % 2][:, :])
            slab_mm8.append(mm.v)
        S.wait("sync", ast[0], ast[0].v)
        S.wait("sync", ast[1], ast[1].v)
        S.emit("p8")
        p8.close()
        emit_dram_tap("t_attn", attn_st, DM, T)

    pw0 = ExitStack()
    if BUILD_UPTO >= 10:
        w0_buf = pw0.enter_context(nc.sbuf_tensor("w0_buf", [128, 2 * KD * 512], F16))
        H0 = KD * 512
    if BUILD_UPTO >= 9:
        p9 = ExitStack()
        atm = [p9.enter_context(nc.sbuf_tensor(f"atm{i}", [128, DM], F16)) for i in range(2)]
        xr = [p9.enter_context(nc.sbuf_tensor(f"xr{i}", [128, DM], F32)) for i in range(2)]
        nrm = p9.enter_context(nc.sbuf_tensor("nrm", [128, DM], F32))
        x2t = p9.enter_context(nc.sbuf_tensor("x2t", [128, DM], F32))
        x2n16 = p9.enter_context(nc.sbuf_tensor("x2n16", [128, DM], F16))
        scr9 = p9.enter_context(nc.sbuf_tensor("scr9", [128, DM], F16))
        cl9 = p9.enter_context(nc.sbuf_tensor("cl9", [128, 8], F32))
        st9 = {}
        dve_done9 = {}
        for i in range(8):
            if i >= 2:
                S.wait("sync", vv, dve_done9.get(i - 2, 0))
            w_a = S.dma("sync", xq[i % 2], atm[i % 2][:, :],
                        attn_st[:, CH * i:CH * (i + 1)], transpose=True)
            w_x = S.dma("sync", xq[i % 2], xr[i % 2][:, :], x_in[CH * i:CH * (i + 1), :])
            S.wait("scalar", xq[i % 2], w_a)
            S.wait("scalar", ast[i % 2], st9.get(i - 2, 0))
            w_s1 = S.op("scalar", _f(lambda e, i=i: e.activation(
                scr9[:, :], atm[i % 2][:, :], AF.Square, accum_out=cl9[:, 1:2])), ssc)
            S.wait("vector", xq[i % 2], w_a)
            w_rs9 = S.op("vector", _f(lambda e, i=i: e.reduce_sum(cl9[:, 0:1], atm[i % 2][:, :], axis=AX.X)), vv)
            S.wait("vector", vv, w_rs9)
            S.op("vector", _f(lambda e: e.tensor_single_scalar(cl9[:, 2:3], cl9[:, 0:1], 1.0 / DM, ALU.mult)))
            S.wait("vector", ssc, w_s1)
            S.op("vector", _f(lambda e: e.tensor_single_scalar(cl9[:, 3:4], cl9[:, 1:2], 1.0 / DM, ALU.mult)))
            w_m29 = S.op("vector", _f(lambda e: e.tensor_tensor(cl9[:, 4:5], cl9[:, 2:3], cl9[:, 2:3], ALU.mult)), vv)
            S.wait("vector", vv, w_m29)
            w_v9 = S.op("vector", _f(lambda e: e.tensor_tensor(cl9[:, 4:5], cl9[:, 3:4], cl9[:, 4:5], ALU.subtract)), vv)
            S.wait("scalar", vv, w_v9)
            w_sq9 = S.op("scalar", _f(lambda e: e.activation(cl9[:, 5:6], cl9[:, 4:5], AF.Sqrt, bias=eps_col[:, 0:1])), ssc)
            S.wait("vector", ssc, w_sq9)
            w_rc9 = S.op("vector", _f(lambda e: e.reciprocal(cl9[:, 6:7], cl9[:, 5:6])), vv)
            S.wait("vector", vv, w_rc9)
            S.op("vector", _f(lambda e, i=i: e.tensor_scalar(
                nrm[:, :], atm[i % 2][:, :], cl9[:, 2:3], cl9[:, 6:7], ALU.subtract, ALU.mult)))
            S.wait("vector", xq[i % 2], w_x)
            S.wait("vector", x2q, x2q.v)
            w_x2 = S.op("vector", _f(lambda e, i=i: e.tensor_add(
                x2t[:, :], xr[i % 2][:, :], nrm[:, :])), vv)
            S.wait("sync", vv, w_x2)
            S.dma("sync", x2q, x2_st[CH * i:CH * (i + 1), :], x2t[:, :])
            S.wait("scalar", vv, w_x2)
            w_s2 = S.op("scalar", _f(lambda e: e.activation(
                scr9[:, :], x2t[:, :], AF.Square, accum_out=cl9[:, 1:2])), ssc)
            w_rs92 = S.op("vector", _f(lambda e: e.reduce_sum(cl9[:, 0:1], x2t[:, :], axis=AX.X)), vv)
            S.wait("vector", vv, w_rs92)
            S.op("vector", _f(lambda e: e.tensor_single_scalar(cl9[:, 2:3], cl9[:, 0:1], 1.0 / DM, ALU.mult)))
            S.wait("vector", ssc, w_s2)
            S.op("vector", _f(lambda e: e.tensor_single_scalar(cl9[:, 3:4], cl9[:, 1:2], 1.0 / DM, ALU.mult)))
            w_m292 = S.op("vector", _f(lambda e: e.tensor_tensor(cl9[:, 4:5], cl9[:, 2:3], cl9[:, 2:3], ALU.mult)), vv)
            S.wait("vector", vv, w_m292)
            w_v92 = S.op("vector", _f(lambda e: e.tensor_tensor(cl9[:, 4:5], cl9[:, 3:4], cl9[:, 4:5], ALU.subtract)), vv)
            S.wait("scalar", vv, w_v92)
            w_sq92 = S.op("scalar", _f(lambda e: e.activation(cl9[:, 5:6], cl9[:, 4:5], AF.Sqrt, bias=eps_col[:, 0:1])), ssc)
            S.wait("vector", ssc, w_sq92)
            w_rc92 = S.op("vector", _f(lambda e: e.reciprocal(cl9[:, 6:7], cl9[:, 5:6])), vv)
            S.wait("vector", vv, w_rc92)
            S.wait("vector", ast[(i - 1) % 2], st9.get(i - 1, 0))
            w_n2 = S.op("vector", _f(lambda e: e.tensor_scalar(
                x2n16[:, :], x2t[:, :], cl9[:, 2:3], cl9[:, 6:7], ALU.subtract, ALU.mult)), vv)
            S.wait("sync", vv, w_n2)
            dve_done9[i] = w_n2
            st9[i] = S.dma("sync", ast[i % 2], x2n_st[CH * i:CH * (i + 1), :], x2n16[:, :])
        S.wait("sync", ast[0], ast[0].v)
        S.wait("sync", ast[1], ast[1].v)
        S.wait("sync", x2q, x2q.v)
        w0_pre = []
        if BUILD_UPTO >= 10:
            for sb in range(2):
                for k0 in (0, 8):
                    w_p = S.dma("gpsimd", gqh[sb % 2],
                                bass.AP(w0_buf, (sb % 2) * H0 + k0 * 512,
                                        [[2 * H0, 128], [512, 8], [1, 512]]),
                                bass.AP(w0_T, k0 * 128 * 8192 + 512 * sb,
                                        [[8192, 128], [8192 * 128, 8], [1, 512]]))
                w0_pre.append(w_p)
        S.emit("p9")
        p9.close()
        emit_dram_tap("t_x2", x2_st, T, DM, f16=False)

    if BUILD_UPTO >= 10:
        p10 = ExitStack()
        d_fm = p10.enter_context(nc.sbuf_tensor("d_fm", [128, 64 * T], F16))
        p10w = ExitStack()
        x2nfm = p10w.enter_context(nc.sbuf_tensor("x2nfm", [128, KD * T], F16))
        for k in range(KD):
            w_xl = S.dma("sync", bq, x2nfm[:, T * k:T * (k + 1)],
                         x2n_st[:, 128 * k:128 * (k + 1)], transpose=True)
        slab_mm0 = []
        ev10 = {}
        for sb in range(16):
            half = sb % 2
            if sb >= 2:
                S.wait("gpsimd", mm, slab_mm0[sb - 2])
                for k0 in (0, 8):
                    w_sl0 = S.dma("gpsimd", gqh[half],
                                  bass.AP(w0_buf, half * H0 + k0 * 512,
                                          [[2 * H0, 128], [512, 8], [1, 512]]),
                                  bass.AP(w0_T, k0 * 128 * 8192 + 512 * sb,
                                          [[8192, 128], [8192 * 128, 8], [1, 512]]))
            else:
                w_sl0 = w0_pre[sb]
            S.wait("tensor", gqh[half], w_sl0)
            if sb == 0:
                S.wait("tensor", bq, w_xl)
            for ncc in range(4):
                n_g = 4 * sb + ncc
                npar = n_g % 2
                for tt in range(2):
                    S.wait("tensor", ssc, ev10.get((n_g - 2) * 2 + tt, 0))
                for k in range(KD):
                    for tt in range(2):
                        def m0(e, half=half, k=k, ncc=ncc, tt=tt, npar=npar):
                            return e.matmul(
                                banks[2 * npar + tt][0:128, :],
                                bass.AP(w0_buf, half * H0 + k * 512 + ncc * 128,
                                        [[2 * H0, 128], [1, 128]]),
                                x2nfm[:, T * k + 512 * tt:T * k + 512 * tt + 512],
                                start=(k == 0), stop=(k == KD - 1))
                        if k == KD - 1:
                            w_m0 = S.op("tensor", m0, mm)
                        else:
                            S.op("tensor", m0)
                S.wait("scalar", mm, w_m0)
                for tt in range(2):
                    ev10[n_g * 2 + tt] = S.op("scalar", _f(
                        lambda e, n_g=n_g, tt=tt, npar=npar: e.activation(
                            d_fm[:, T * n_g + 512 * tt:T * n_g + 512 * tt + 512],
                            banks[2 * npar + tt][0:128, :], AF.Gelu,
                            bias=b0_cols[:, n_g:n_g + 1])), ssc)
            slab_mm0.append(mm.v)
        S.emit("p10")
        p10w.close()

        p11 = ExitStack()
        w1_buf = p11.enter_context(nc.sbuf_tensor("w1_buf", [128, 2 * 8 * 512], F16))
        estg = [p11.enter_context(nc.sbuf_tensor(f"estg{i}", [128, T], F16)) for i in range(2)]
        H1 = 8 * 512
        d_last = ev10[127]
        sto11 = {}
        prev_pass_mm = 0
        for ps in range(4):
            slab_mm1 = []
            for kg in range(8):
                half = kg % 2
                if kg >= 2:
                    S.wait("gpsimd", mm, slab_mm1[kg - 2])
                elif ps >= 1:
                    S.wait("gpsimd", mm, prev_pass_mm)
                w_sl1 = S.dma("gpsimd", gqh[half],
                              bass.AP(w1_buf, half * H1,
                                      [[2 * H1, 128], [512, 8], [1, 512]]),
                              bass.AP(w1_T, (8 * kg) * 128 * DM + 512 * ps,
                                      [[DM, 128], [DM * 128, 8], [1, 512]]))
                S.wait("tensor", gqh[half], w_sl1)
                if ps == 0 and kg == 0:
                    S.wait("tensor", ssc, d_last)
                for kl in range(8):
                    k = 8 * kg + kl
                    for fo in range(4):
                        for tt in range(2):
                            def m1(e, half=half, kl=kl, fo=fo, tt=tt, k=k):
                                return e.matmul(
                                    banks[fo * 2 + tt][0:128, :],
                                    bass.AP(w1_buf, half * H1 + kl * 512 + fo * 128,
                                            [[2 * H1, 128], [1, 128]]),
                                    d_fm[:, T * k + 512 * tt:T * k + 512 * tt + 512],
                                    start=(k == 0), stop=(k == 63))
                            if kl == 7 and fo == 3 and tt == 1:
                                w_m1 = S.op("tensor", m1, mm)
                            else:
                                S.op("tensor", m1)
                slab_mm1.append(mm.v)
            S.wait("vector", mm, w_m1)
            for fo in range(4):
                f_g = 4 * ps + fo
                S.wait("vector", gst[f_g % 2], sto11.get(f_g - 2, 0))
                for tt in range(2):
                    w_e = S.op("vector", _f(lambda e, fo=fo, tt=tt, f_g=f_g: e.tensor_scalar_add(
                        estg[fo % 2][:, 512 * tt:512 * (tt + 1)],
                        banks[fo * 2 + tt][0:128, :],
                        b1_cols[:, f_g:f_g + 1])), vv)
                S.wait("sync", vv, w_e)
                sto11[f_g] = S.dma("sync", gst[f_g % 2], e_st[128 * f_g:128 * (f_g + 1), :],
                                   estg[fo % 2][:, :])
            S.wait("tensor", vv, vv.v)
            prev_pass_mm = mm.v
        S.wait("sync", gst[0], gst[0].v)
        S.wait("sync", gst[1], gst[1].v)
        S.emit("p11")
        emit_dram_tap("t_e", e_st, DM, T)
        p11.close()
        p10.close()
        pw0.close()

    if BUILD_UPTO >= 12:
        p13 = ExitStack()
        etm = [p13.enter_context(nc.sbuf_tensor(f"etm{i}", [128, DM], F16)) for i in range(2)]
        x2r = [p13.enter_context(nc.sbuf_tensor(f"x2r{i}", [128, DM], F32)) for i in range(2)]
        nrm3 = p13.enter_context(nc.sbuf_tensor("nrm3", [128, DM], F32))
        outt = p13.enter_context(nc.sbuf_tensor("outt", [128, DM], F32))
        scr13 = p13.enter_context(nc.sbuf_tensor("scr13", [128, DM], F16))
        cl13 = p13.enter_context(nc.sbuf_tensor("cl13", [128, 8], F32))
        st13 = {}
        dve13 = {}
        for i in range(8):
            if i >= 2:
                S.wait("sync", vv, dve13.get(i - 2, 0))
            w_e = S.dma("sync", xq[i % 2], etm[i % 2][:, :], e_st[:, CH * i:CH * (i + 1)], transpose=True)
            w_x = S.dma("sync", xq[i % 2], x2r[i % 2][:, :], x2_st[CH * i:CH * (i + 1), :])
            S.wait("scalar", xq[i % 2], w_e)
            w_s1 = S.op("scalar", _f(lambda e, i=i: e.activation(
                scr13[:, :], etm[i % 2][:, :], AF.Square, accum_out=cl13[:, 1:2])), ssc)
            S.wait("vector", xq[i % 2], w_e)
            w_rs13 = S.op("vector", _f(lambda e, i=i: e.reduce_sum(cl13[:, 0:1], etm[i % 2][:, :], axis=AX.X)), vv)
            S.wait("vector", vv, w_rs13)
            S.op("vector", _f(lambda e: e.tensor_single_scalar(cl13[:, 2:3], cl13[:, 0:1], 1.0 / DM, ALU.mult)))
            S.wait("vector", ssc, w_s1)
            S.op("vector", _f(lambda e: e.tensor_single_scalar(cl13[:, 3:4], cl13[:, 1:2], 1.0 / DM, ALU.mult)))
            w_m213 = S.op("vector", _f(lambda e: e.tensor_tensor(cl13[:, 4:5], cl13[:, 2:3], cl13[:, 2:3], ALU.mult)), vv)
            S.wait("vector", vv, w_m213)
            w_v = S.op("vector", _f(lambda e: e.tensor_tensor(cl13[:, 4:5], cl13[:, 3:4], cl13[:, 4:5], ALU.subtract)), vv)
            S.wait("scalar", vv, w_v)
            w_sq13 = S.op("scalar", _f(lambda e: e.activation(cl13[:, 5:6], cl13[:, 4:5], AF.Sqrt, bias=eps_col[:, 0:1])), ssc)
            S.wait("vector", ssc, w_sq13)
            w_rc13 = S.op("vector", _f(lambda e: e.reciprocal(cl13[:, 6:7], cl13[:, 5:6])), vv)
            S.wait("vector", vv, w_rc13)
            S.op("vector", _f(lambda e, i=i: e.tensor_scalar(
                nrm3[:, :], etm[i % 2][:, :], cl13[:, 2:3], cl13[:, 6:7], ALU.subtract, ALU.mult)))
            S.wait("vector", xq[i % 2], w_x)
            S.wait("vector", st, st.v)
            w_o = S.op("vector", _f(lambda e, i=i: e.tensor_add(
                outt[:, :], x2r[i % 2][:, :], nrm3[:, :])), vv)
            S.wait("sync", vv, w_o)
            dve13[i] = w_o
            st13[i] = S.dma("sync", st, y_out[CH * i:CH * (i + 1), :], outt[:, :])
        S.wait("sync", st, st.v)
        S.emit("p13")
        p13.close()

    return nc


def kernel(L0_attn_hidden_states, L0_attn_in_proj_w, L0_attn_conv_w, L0_attn_conv_b,
           L0_attn_dt_bias, L0_attn_A_log, L0_attn_D, L0_attn_norm_w,
           L0_attn_out_proj_w, L0_ffn_W0, L0_ffn_bias0, L0_ffn_W1, L0_ffn_bias1):
    x = np.asarray(L0_attn_hidden_states, dtype=np.float32)
    w_in_T = np.ascontiguousarray(np.asarray(L0_attn_in_proj_w, np.float32).T)
    w_out_T = np.ascontiguousarray(np.asarray(L0_attn_out_proj_w, np.float32).T)
    w0_T = np.ascontiguousarray(np.asarray(L0_ffn_W0, np.float32).T)
    w1_T = np.ascontiguousarray(np.asarray(L0_ffn_W1, np.float32).T)

    nc = build()
    in_maps = []
    for core in range(8):
        b, h = core // 2, core % 2
        shard = np.ascontiguousarray(x[b, T * h:T * (h + 1), :])
        flags = np.array([1.0 - h, float(h)], np.float32)
        in_maps.append({
            "x_in": shard, "w_in_T": w_in_T, "w_out_T": w_out_T,
            "w0_T": w0_T, "w1_T": w1_T,
            "conv_w": np.asarray(L0_attn_conv_w, np.float32),
            "conv_b": np.asarray(L0_attn_conv_b, np.float32),
            "dt_bias": np.asarray(L0_attn_dt_bias, np.float32),
            "A_log": np.asarray(L0_attn_A_log, np.float32),
            "Dp": np.asarray(L0_attn_D, np.float32),
            "norm_w": np.asarray(L0_attn_norm_w, np.float32),
            "b0": np.asarray(L0_ffn_bias0, np.float32),
            "b1": np.asarray(L0_ffn_bias1, np.float32),
            "flags": flags,
        })
    res = run_bass_kernel_spmd(nc, in_maps, core_ids=list(range(8)))
    out = np.empty((4, 2048, DM), np.float32)
    for core in range(8):
        b, h = core // 2, core % 2
        out[b, T * h:T * (h + 1), :] = res.results[core]["y_out"]
    return out


# version 09 (final)
# speedup vs baseline: 1.0913x; 1.0913x over previous
import numpy as np
import ml_dtypes
import concourse.bass as bass
import concourse.mybir as mybir
from concourse.bass_utils import run_bass_kernel_spmd

F32 = mybir.dt.float32
F16 = mybir.dt.float16
ALU = mybir.AluOpType
AF = mybir.ActivationFunctionType
AX = mybir.AxisListType

T = 1024
DM = 2048
DIN = 4096
NST = 128
CONVD = DIN + 2 * NST
PROJ = DIN + CONVD + 64
NH = 64
HPD = 64
CH = 128
NCHUNK = T // CH
EPS = 1e-5
KD = DM // 128
NXBC = CONVD // 128

BUILD_UPTO = 99
DEBUG_TAPS = []


class Ctr:
    def __init__(self, stack, nc, name, dma=False):
        self.h = stack.enter_context(nc.semaphore(name))
        self.v = 0
        self.dma = dma


class Sched:

    ENGS = ("sync", "gpsimd", "tensor", "vector", "scalar")

    def __init__(self, nc):
        self.nc = nc
        self.q = {e: [] for e in self.ENGS}

    def on(self, eng, fn):
        self.q[eng].append(fn)

    def wait(self, eng, ctr, val):
        if ctr.dma:
            val = ctr.v
        if val > 0:
            self.q[eng].append(lambda e, h=ctr.h, v=val: e.wait_ge(h, v))

    def op(self, eng, fn, ctr=None, n=1):
        if ctr is None:
            self.q[eng].append(lambda e: fn(e))
            return None
        ctr.v += n
        self.q[eng].append(lambda e, h=ctr.h, k=n: fn(e).then_inc(h, k))
        return ctr.v

    def dma(self, eng, ctr, out, in_, **kw):
        ctr.v += 16
        self.q[eng].append(
            lambda e, h=ctr.h: e.dma_start(out=out, in_=in_, **kw).then_inc(h, 16))
        return ctr.v

    def emit(self, name=""):
        from contextlib import nullcontext
        scope = self.nc.named_scope(name) if name else nullcontext()
        with scope, self.nc.Block() as blk:
            for eng in self.ENGS:
                ops = self.q[eng]
                if not ops:
                    continue

                def body(e, ops=ops):
                    for f in ops:
                        f(e)
                getattr(blk, eng)(body)
        self.q = {e: [] for e in self.ENGS}


def _bcast_hp(tile_ap, nh, np_):
    return tile_ap.unsqueeze(2).broadcast_to([128, nh, np_])


def _bcast_outer(tile_ap, nouter):
    return tile_ap.unsqueeze(1).broadcast_to([128, nouter, tile_ap.shape[1]])


def build():
    from contextlib import ExitStack

    nc = bass.Bass("TRN2", target_bir_lowering=False,
                   detect_race_conditions=False)

    x_in = nc.dram_tensor("x_in", [T, DM], F32, kind="ExternalInput")
    w_in_T = nc.dram_tensor("w_in_T", [DM, PROJ], F32, kind="ExternalInput")
    w_out_T = nc.dram_tensor("w_out_T", [DIN, DM], F32, kind="ExternalInput")
    w0_T = nc.dram_tensor("w0_T", [DM, 8192], F32, kind="ExternalInput")
    w1_T = nc.dram_tensor("w1_T", [8192, DM], F32, kind="ExternalInput")
    conv_w = nc.dram_tensor("conv_w", [CONVD, 4], F32, kind="ExternalInput")
    conv_b = nc.dram_tensor("conv_b", [CONVD], F32, kind="ExternalInput")
    dt_bias = nc.dram_tensor("dt_bias", [NH], F32, kind="ExternalInput")
    A_log = nc.dram_tensor("A_log", [NH], F32, kind="ExternalInput")
    Dp = nc.dram_tensor("Dp", [NH], F32, kind="ExternalInput")
    norm_w = nc.dram_tensor("norm_w", [DIN], F32, kind="ExternalInput")
    b0 = nc.dram_tensor("b0", [8192], F32, kind="ExternalInput")
    b1 = nc.dram_tensor("b1", [DM], F32, kind="ExternalInput")
    flags = nc.dram_tensor("flags", [2], F32, kind="ExternalInput")
    y_out = nc.dram_tensor("y_out", [T, DM], F32, kind="ExternalOutput")

    xn_st = nc.dram_tensor("xn_st", [T, DM], F16)
    act_st = nc.dram_tensor("act_st", [CONVD, T], F16)
    gate_st = nc.dram_tensor("gate_st", [DIN, T], F16)
    dAc_st = nc.dram_tensor("dAc_st", [NH, T], F32)
    dasum_st = nc.dram_tensor("dasum_st", [NCHUNK, NH], F16)
    tmq_st = nc.dram_tensor("tmq_st", [4 * NH, T], F16)
    yn_st = nc.dram_tensor("yn_st", [T, DIN], F16)
    attn_st = nc.dram_tensor("attn_st", [DM, T], F16)
    x2_st = nc.dram_tensor("x2_st", [T, DM], F32)
    x2n_st = nc.dram_tensor("x2n_st", [T, DM], F16)
    e_st = nc.dram_tensor("e_st", [DM, T], F16)
    cc1_in = nc.dram_tensor("cc1_in", [CONVD, 3], F32)
    cc1_out = nc.dram_tensor("cc1_out", [CONVD, 3], F32)
    cc2_in = nc.dram_tensor("cc2_in", [NST, DIN], F32)
    cc2_out = nc.dram_tensor("cc2_out", [NST, DIN], F32)

    taps = {}
    tapshape = {
        "t_proj": [PROJ, T], "t_act": [CONVD, T], "t_dt": [NH, T],
        "t_dAc": [NH, T], "t_y": [T, DIN], "t_yn": [T, DIN],
        "t_attn": [DM, T], "t_x2": [T, DM], "t_S": [NST, DIN],
        "t_xn": [T, DM], "t_gate": [DIN, T], "t_x2n": [T, DM],
        "t_e": [DM, T],
    }
    for name in DEBUG_TAPS:
        if name in tapshape:
            taps[name] = nc.dram_tensor(name, tapshape[name], F32, kind="ExternalOutput")

    es = ExitStack()

    flagA = nc.alloc_sbuf_tensor("flagA", [128, 1], F32)
    flagB = nc.alloc_sbuf_tensor("flagB", [128, 1], F32)
    A_col = nc.alloc_sbuf_tensor("A_col", [NH, 1], F32)
    dtb_col = nc.alloc_sbuf_tensor("dtb_col", [NH, 1], F32)
    D_rep = nc.alloc_sbuf_tensor("D_rep", [128, NH], F32)
    cw_cols = nc.alloc_sbuf_tensor("cw_cols", [128, NXBC * 4], F32)
    cb_cols = nc.alloc_sbuf_tensor("cb_cols", [128, NXBC], F32)
    b0_cols = nc.alloc_sbuf_tensor("b0_cols", [128, 64], F32)
    b1_cols = nc.alloc_sbuf_tensor("b1_cols", [128, 16], F32)
    eps_col = nc.alloc_sbuf_tensor("eps_col", [128, 1], F32)
    tril01 = nc.alloc_sbuf_tensor("tril01", [128, 128], F16)
    ident32 = nc.alloc_sbuf_tensor("ident32", [128, 128], F32)
    ones64 = nc.alloc_sbuf_tensor("ones64", [NH, T], F32)
    stash_first = nc.alloc_sbuf_tensor("stash_first", [128, NXBC * 3], F16)
    stash_halo = nc.alloc_sbuf_tensor("stash_halo", [128, NXBC * 3], F32)
    fixpack = nc.alloc_sbuf_tensor("fixpack", [128, NXBC * 3], F32)
    fixp16 = nc.alloc_sbuf_tensor("fixp16", [128, NXBC * 3], F16)

    sq = Ctr(es, nc, "sq", dma=True)
    gq = Ctr(es, nc, "gq", dma=True)
    st = Ctr(es, nc, "st", dma=True)
    cq = Ctr(es, nc, "cq", dma=True)
    xq = [Ctr(es, nc, f"xq{i}", dma=True) for i in range(2)]
    rqs = [Ctr(es, nc, f"rqs{i}", dma=True) for i in range(2)]
    bq = Ctr(es, nc, "bq", dma=True)
    gqh = [Ctr(es, nc, f"gqh{i}", dma=True) for i in range(2)]
    gst = [Ctr(es, nc, f"gst{i}", dma=True) for i in range(2)]
    ast = [Ctr(es, nc, f"ast{i}", dma=True) for i in range(2)]
    x2q = Ctr(es, nc, "x2q", dma=True)
    pq = Ctr(es, nc, "pq")
    mm = Ctr(es, nc, "mm")
    vv = Ctr(es, nc, "vv")
    ssc = Ctr(es, nc, "ssc")
    ccs = Ctr(es, nc, "ccs")

    banks = [es.enter_context(nc.psum_tensor(f"pb{i}", [128, 512], F32))
             for i in range(8)]

    S = Sched(nc)


    def emit_dram_tap(tap_name, src_dram, rows, cols, f16=True):
        if tap_name not in taps:
            return
        with nc.sbuf_tensor(tap_name + "_stg", [128, cols], F32) as stg:
            for r in range(rows // 128):
                if f16:
                    S.wait("gpsimd", st, st.v)
                    w_l = S.dma("gpsimd", gq, stg[:, :], src_dram[128 * r:128 * (r + 1), :])
                    S.wait("sync", gq, w_l)
                else:
                    S.wait("sync", st, st.v)
                    w_l = S.dma("sync", sq, stg[:, :], src_dram[128 * r:128 * (r + 1), :])
                    S.wait("sync", sq, w_l)
                S.dma("sync", st, taps[tap_name][128 * r:128 * (r + 1), :], stg[:, :])
            S.wait("sync", st, st.v)
            S.emit("tap_" + tap_name)

    S.dma("sync", cq, flagA[:, :], bass.AP(flags, 0, [[0, 128], [1, 1]]))
    S.dma("sync", cq, flagB[:, :], bass.AP(flags, 1, [[0, 128], [1, 1]]))
    w_acol = S.dma("sync", cq, A_col[:, :], bass.AP(A_log, 0, [[1, NH], [1, 1]]))
    S.dma("sync", cq, dtb_col[:, :], bass.AP(dt_bias, 0, [[1, NH], [1, 1]]))
    S.dma("sync", cq, D_rep[:, :], bass.AP(Dp, 0, [[0, 128], [1, NH]]))
    S.dma("sync", cq, cw_cols[:, :], bass.AP(conv_w, 0, [[4, 128], [512, NXBC], [1, 4]]))
    w_consts = cq.v

    def _f(fn):
        return fn

    S.op("gpsimd", lambda g: g.memset(eps_col[:, :], EPS))
    w_m1_ = S.op("gpsimd", lambda g: g.memset(tril01[:, :], 1.0), pq)
    S.wait("gpsimd", pq, w_m1_)
    S.op("gpsimd", lambda g: g.affine_select(
        out=tril01[:, :], in_=tril01[:, :], compare_op=ALU.is_ge, fill=0.0,
        base=0, pattern=[[1, 128]], channel_multiplier=-1))
    w_m2_ = S.op("gpsimd", lambda g: g.memset(ident32[:, :], 0.0), pq)
    S.wait("gpsimd", pq, w_m2_)
    S.op("gpsimd", lambda g: g.affine_select(
        out=ident32[:, :], in_=ident32[:, :], compare_op=ALU.not_equal, fill=1.0,
        base=0, pattern=[[-1, 128]], channel_multiplier=1))
    w_msk = S.op("gpsimd", lambda g: g.memset(ones64[:, :], 1.0), pq)

    S.wait("scalar", cq, w_acol)
    w_expA = S.op("scalar", lambda e: e.activation(A_col[:, :], A_col[:, :], AF.Exp), ssc)
    S.wait("vector", ssc, w_expA)
    S.op("vector", lambda e: e.tensor_single_scalar(A_col[:, :], A_col[:, :], -1.0, ALU.mult), vv)
    S.emit("p0")

    with nc.sbuf_tensor("bias_ld", [64, 128], F32) as bias_ld:
        w_b0a = S.dma("sync", cq, bias_ld[:, :], b0[:].rearrange("(a b) -> a b", b=128))
        S.wait("tensor", cq, w_b0a)
        w_t0 = S.op("tensor", lambda e: e.transpose(
            banks[1][0:128, 0:64], bias_ld[:, :], ident32[0:64, 0:64]), mm)
        S.wait("vector", mm, w_t0)
        w_c0 = S.op("vector", lambda e: e.tensor_copy(b0_cols[:, :], banks[1][0:128, 0:64]), vv)
        S.wait("sync", vv, w_c0)
        w_b1a = S.dma("sync", cq, bias_ld[0:16, :], b1[:].rearrange("(a b) -> a b", b=128))
        S.wait("tensor", cq, w_b1a)
        S.wait("tensor", vv, w_c0)
        w_t1 = S.op("tensor", lambda e: e.transpose(
            banks[1][0:128, 0:16], bias_ld[0:16, :], ident32[0:16, 0:16]), mm)
        S.wait("vector", mm, w_t1)
        w_c1 = S.op("vector", lambda e: e.tensor_copy(b1_cols[:, :], banks[1][0:128, 0:16]), vv)
        S.wait("sync", vv, w_c1)
        w_cba = S.dma("sync", cq, bias_ld[0:NXBC, :], conv_b[:].rearrange("(a b) -> a b", b=128))
        S.wait("tensor", cq, w_cba)
        S.wait("tensor", vv, w_c1)
        w_t2 = S.op("tensor", lambda e: e.transpose(
            banks[1][0:128, 0:NXBC], bias_ld[0:NXBC, :], ident32[0:NXBC, 0:NXBC]), mm)
        S.wait("vector", mm, w_t2)
        S.op("vector", lambda e: e.tensor_copy(cb_cols[:, :], banks[1][0:128, 0:NXBC]), vv)
        S.emit("p0bias")



    pB1 = ExitStack()
    BC_fm = pB1.enter_context(nc.sbuf_tensor("BC_fm", [128, 2 * T], F16))
    dAc_tm = pB1.enter_context(nc.sbuf_tensor("dAc_tm", [128, NCHUNK * NH], F32))
    tmq_tm = pB1.enter_context(nc.sbuf_tensor("tmq_tm", [128, NCHUNK * 4 * NH], F16))
    decay_reps = pB1.enter_context(nc.sbuf_tensor("decay_reps", [128, NCHUNK * NH], F32))
    dt_raw = pB1.enter_context(nc.sbuf_tensor("dt_raw", [NH, T], F32))
    pdt2 = ExitStack()
    r_t = pdt2.enter_context(nc.sbuf_tensor("r_t", [NH, T], F32))
    sA = pdt2.enter_context(nc.sbuf_tensor("sA", [NH, T], F32))
    sB = pdt2.enter_context(nc.sbuf_tensor("sB", [NH, T], F32))
    dAc_l = pdt2.enter_context(nc.sbuf_tensor("dAc_l", [NH, T], F32))
    e16 = [pdt2.enter_context(nc.sbuf_tensor(f"e16_{i}", [NH, T], F16)) for i in range(4)]
    ds16 = pdt2.enter_context(nc.sbuf_tensor("ds16", [NH, T], F16))
    dargs = pdt2.enter_context(nc.sbuf_tensor("dargs", [128, NCHUNK * NH], F16))

    pA = ExitStack()
    xnorm_fm = pA.enter_context(nc.sbuf_tensor("xnorm_fm", [128, KD * T], F16))
    if BUILD_UPTO >= 1:
        p1 = ExitStack()
        xt = [p1.enter_context(nc.sbuf_tensor(f"xt{i}", [128, DM], F32)) for i in range(2)]
        xn_t = [p1.enter_context(nc.sbuf_tensor(f"xn_t{i}", [128, DM], F16)) for i in range(2)]
        sq_scr = p1.enter_context(nc.sbuf_tensor("sq_scr", [128, DM], F16))
        cl = p1.enter_context(nc.sbuf_tensor("cl", [128, 8], F32))

        store_done = [0] * 8
        for i in range(8):
            if i >= 2:
                S.wait("sync", gst[i % 2], store_done[i - 2])
            w_ld = S.dma("sync", xq[i % 2], xt[i % 2][:, :], x_in[128 * i:128 * (i + 1), :])
            S.wait("vector", xq[i % 2], w_ld)
            w_rs = S.op("vector", _f(lambda e, i=i: e.reduce_sum(cl[:, 0:1], xt[i % 2][:, :], axis=AX.X)), vv)
            S.wait("vector", vv, w_rs)
            S.op("vector", _f(lambda e: e.tensor_single_scalar(cl[:, 2:3], cl[:, 0:1], 1.0 / DM, ALU.mult)), vv)
            S.wait("scalar", xq[i % 2], w_ld)
            w_sq = S.op("scalar", _f(lambda e, i=i: e.activation(
                sq_scr[:, :], xt[i % 2][:, :], AF.Square, accum_out=cl[:, 1:2])), ssc)
            S.wait("vector", ssc, w_sq)
            S.op("vector", _f(lambda e: e.tensor_single_scalar(cl[:, 3:4], cl[:, 1:2], 1.0 / DM, ALU.mult)))
            w_m2 = S.op("vector", _f(lambda e: e.tensor_tensor(cl[:, 4:5], cl[:, 2:3], cl[:, 2:3], ALU.mult)), vv)
            S.wait("vector", vv, w_m2)
            w_var = S.op("vector", _f(lambda e: e.tensor_tensor(cl[:, 4:5], cl[:, 3:4], cl[:, 4:5], ALU.subtract)), vv)
            S.wait("scalar", vv, w_var)
            w_sqt = S.op("scalar", _f(lambda e: e.activation(cl[:, 5:6], cl[:, 4:5], AF.Sqrt, bias=eps_col[:, 0:1])), ssc)
            S.wait("vector", ssc, w_sqt)
            w_rc = S.op("vector", _f(lambda e: e.reciprocal(cl[:, 6:7], cl[:, 5:6])), vv)
            S.wait("vector", vv, w_rc)
            w_n = S.op("vector", _f(lambda e, i=i: e.tensor_scalar(
                xn_t[i % 2][:, :], xt[i % 2][:, :], cl[:, 2:3], cl[:, 6:7],
                ALU.subtract, ALU.mult)), vv)
            S.wait("sync", vv, w_n)
            store_done[i] = S.dma("sync", gst[i % 2], xn_st[128 * i:128 * (i + 1), :], xn_t[i % 2][:, :])
        S.wait("sync", gst[0], store_done[6])
        S.wait("sync", gst[1], store_done[7])
        if "t_dbg" in DEBUG_TAPS:
            dbg_xt = nc.dram_tensor("dbg_xt", [128, DM], F32, kind="ExternalOutput")
            dbg_xn = nc.dram_tensor("dbg_xn", [128, DM], F32, kind="ExternalOutput")
            dbg_cl = nc.dram_tensor("dbg_cl", [128, 8], F32, kind="ExternalOutput")
            S.dma("sync", st, dbg_xt[:, :], xt[1][:, :])
            S.dma("sync", st, dbg_cl[:, :], cl[:, :])
            w_cv = S.op("vector", lambda e: e.tensor_copy(sq_scr[:, :], xn_t[1][:, :]), vv)
            S.wait("vector", ssc, ssc.v)
            S.wait("sync", vv, w_cv)
            S.op("vector", lambda e: e.tensor_copy(xt[0][:, :], sq_scr[:, :]), vv)
            S.wait("sync", vv, vv.v)
            S.dma("sync", st, dbg_xn[:, :], xt[0][:, :])
            S.wait("sync", st, st.v)
        for k in range(KD):
            S.dma("sync", sq, xnorm_fm[:, T * k:T * (k + 1)],
                  xn_st[:, 128 * k:128 * (k + 1)], transpose=True)
        S.emit("p1")
        p1.close()

    if BUILD_UPTO >= 2:
        p2 = ExitStack()
        wbuf = p2.enter_context(nc.sbuf_tensor("wbuf", [128, 2 * KD * 512], F16))
        pool = [p2.enter_context(nc.sbuf_tensor(f"pool{i}", [128, 1027], F16)) for i in range(3)]
        cacc = [p2.enter_context(nc.sbuf_tensor(f"cacc{i}", [128, 1021], F32)) for i in range(2)]
        convout = [p2.enter_context(nc.sbuf_tensor(f"convout{i}", [128, 1021], F32)) for i in range(2)]
        astg = [p2.enter_context(nc.sbuf_tensor(f"astg{i}", [128, 1021], F16)) for i in range(2)]
        gstg = [p2.enter_context(nc.sbuf_tensor(f"gstg{i}", [128, T], F16)) for i in range(2)]

        blocks = ([(4096 + 512 * i, 512, "xbc") for i in range(8)]
                  + [(8192, 256, "xbc"), (8448, 64, "dt")]
                  + [(512 * i, 512, "gate") for i in range(8)])

        HALF = KD * 512

        pair_evac = []
        slab_used = []
        conv_done = {}
        silu_done = {}
        act_store = {}
        gate_store = {}
        pair_idx = 0
        xbc_fc = 0
        gate_gc = 0
        mm_slab_consumed_prev = 0

        for bi, (f0, w, kind) in enumerate(blocks):
            half = bi % 2
            if bi >= 2:
                S.wait("gpsimd", mm, slab_used[bi - 2])
            for k0 in (0, 8):
                srcap = bass.AP(w_in_T, k0 * 128 * PROJ + f0,
                                [[PROJ, 128], [PROJ * 128, 8], [1, w]])
                dst = bass.AP(wbuf, half * HALF + k0 * w,
                              [[2 * HALF, 128], [w, 8], [1, w]])
                w_slab = S.dma("gpsimd", gqh[half], dst, srcap)
            nfc = max(1, w // 128)
            S.wait("tensor", gqh[half], w_slab)
            if bi == 0:
                S.wait("tensor", sq, sq.v)
            for fc in range(nfc):
                fw = min(128, w)
                fcpar = (pair_idx // 2) % 2
                for tt in range(2):
                    if pair_idx >= 4:
                        c_, v_ = pair_evac[pair_idx - 4]
                        S.wait("tensor", c_, v_)
                    pair_idx += 1
                pair_idx -= 2
                for k in range(KD):
                    for tt in range(2):
                        bk = banks[2 * fcpar + tt]
                        def mmfn(e, bk=bk, half=half, k=k, w=w, fc=fc, tt=tt, fw=fw):
                            return e.matmul(
                                bk[0:fw, :],
                                bass.AP(wbuf, half * HALF + k * w + fc * 128,
                                        [[2 * HALF, 128], [1, fw]]),
                                xnorm_fm[:, T * k + 512 * tt: T * k + 512 * tt + 512],
                                start=(k == 0), stop=(k == KD - 1))
                        if k == KD - 1:
                            w_mm = S.op("tensor", mmfn, mm)
                        else:
                            S.op("tensor", mmfn)
                for tt in range(2):
                    bk = banks[2 * fcpar + tt]
                    if kind == "gate":
                        gc = gate_gc
                        if gc >= 2 and tt == 0:
                            S.wait("scalar", gst[gc % 2], gate_store.get(gc - 2, 0))
                        S.wait("scalar", mm, w_mm)
                        ev = S.op("scalar", _f(lambda e, bk=bk, gc=gc, tt=tt: e.activation(
                            gstg[gc % 2][:, 512 * tt:512 * (tt + 1)], bk[:, :], AF.Copy)), ssc)
                        pair_evac.append((ssc, ev))
                        if tt == 1:
                            S.wait("sync", ssc, ev)
                            gate_store[gc] = S.dma(
                                "sync", gst[gc % 2], gate_st[128 * gc:128 * (gc + 1), :], gstg[gc % 2][:, :])
                            gate_gc += 1
                    elif kind == "dt":
                        S.wait("vector", mm, w_mm)
                        ev = S.op("vector", _f(lambda e, bk=bk, tt=tt: e.tensor_copy(
                            dt_raw[:, 512 * tt:512 * (tt + 1)], bk[0:64, :])), vv)
                        pair_evac.append((vv, ev))
                    else:
                        fc_g = xbc_fc
                        if fc_g >= 3 and tt == 0:
                            S.wait("scalar", vv, conv_done.get(fc_g - 3, 0))
                        S.wait("scalar", mm, w_mm)
                        ncols = 512 if fw == 128 else 512
                        ev = S.op("scalar", _f(lambda e, bk=bk, fc_g=fc_g, tt=tt, fw=fw: e.activation(
                            pool[fc_g % 3][0:fw, 3 + 512 * tt:3 + 512 * (tt + 1)],
                            bk[0:fw, :], AF.Copy)), ssc)
                        pair_evac.append((ssc, ev))
                        if tt == 1:
                            S.wait("vector", ssc, ev)
                            if fc_g >= 2:
                                S.wait("vector", ssc, silu_done.get(fc_g - 2, 0))
                            pt = pool[fc_g % 3]
                            co = convout[fc_g % 2]
                            S.op("vector", _f(lambda e, pt=pt, fc_g=fc_g: e.tensor_scalar(
                                cacc[0][:, :], pt[:, 3:1024], cw_cols[:, 4 * fc_g:4 * fc_g + 1],
                                cb_cols[:, fc_g:fc_g + 1], ALU.mult, ALU.add)))
                            S.op("vector", _f(lambda e, pt=pt, fc_g=fc_g: e.scalar_tensor_tensor(
                                cacc[1][:, :], pt[:, 4:1025], cw_cols[:, 4 * fc_g + 1:4 * fc_g + 2],
                                cacc[0][:, :], ALU.mult, ALU.add)))
                            S.op("vector", _f(lambda e, pt=pt, fc_g=fc_g: e.scalar_tensor_tensor(
                                cacc[0][:, :], pt[:, 5:1026], cw_cols[:, 4 * fc_g + 2:4 * fc_g + 3],
                                cacc[1][:, :], ALU.mult, ALU.add)))
                            S.op("vector", _f(lambda e, pt=pt, co=co, fc_g=fc_g: e.scalar_tensor_tensor(
                                co[:, :], pt[:, 6:1027], cw_cols[:, 4 * fc_g + 3:4 * fc_g + 4],
                                cacc[0][:, :], ALU.mult, ALU.add)))
                            S.op("vector", _f(lambda e, pt=pt, fc_g=fc_g: e.tensor_copy(
                                stash_first[:, 3 * fc_g:3 * fc_g + 3], pt[:, 3:6])))
                            cd = S.op("vector", _f(lambda e, pt=pt, fc_g=fc_g: e.tensor_copy(
                                stash_halo[:, 3 * fc_g:3 * fc_g + 3], pt[:, 1024:1027])), vv)
                            conv_done[fc_g] = cd
                            S.wait("scalar", vv, cd)
                            if fc_g >= 2:
                                S.wait("scalar", ast[fc_g % 2], act_store.get(fc_g - 2, 0))
                            sd = S.op("scalar", _f(lambda e, co=co, fc_g=fc_g: e.activation(
                                astg[fc_g % 2][:, :], co[:, :], AF.Silu)), ssc)
                            silu_done[fc_g] = sd
                            if fc_g in (32, 33):
                                S.wait("vector", ssc, sd)
                                S.op("vector", _f(lambda e, fc_g=fc_g: e.tensor_copy(
                                    BC_fm[:, (fc_g - 32) * T + 3:(fc_g - 32) * T + 1024],
                                    astg[fc_g % 2][:, :])), vv)
                            S.wait("sync", ssc, sd)
                            act_store[fc_g] = S.dma(
                                "sync", ast[fc_g % 2],
                                bass.AP(act_st, fc_g * 128 * T + 3, [[T, 128], [1, 1021]]),
                                astg[fc_g % 2][:, :])
                            xbc_fc += 1
                    pair_idx += 1
            slab_used.append(mm.v)
        if "t_gate" in taps:
            with nc.sbuf_tensor("gtap", [128, T], F32) as gtap:
                for gc in range(32):
                    S.wait("gpsimd", st, st.v)
                    w_l = S.dma("gpsimd", gq, gtap[:, :], gate_st[128 * gc:128 * (gc + 1), :])
                    S.wait("sync", gq, w_l)
                    S.dma("sync", st, taps["t_gate"][128 * gc:128 * (gc + 1), :], gtap[:, :])
                S.wait("sync", st, st.v)
                S.emit("tapg")

    if BUILD_UPTO >= 3:

        S.wait("sync", sq, sq.v)
        S.wait("vector", sq, sq.v)
        S.wait("gpsimd", sq, sq.v)
        S.wait("scalar", sq, sq.v)
        S.wait("tensor", sq, sq.v)
        w_r = S.op("vector", lambda e: e.tensor_scalar_add(r_t[:, :], dt_raw[:, :], dtb_col[:, 0:1]), vv)
        S.wait("scalar", vv, w_r)
        S.op("scalar", lambda e: e.activation(sA[:, :], r_t[:, :], AF.Abs))
        S.op("scalar", lambda e: e.activation(sB[:, :], sA[:, :], AF.Exp, scale=-1.0))
        w_ln = S.op("scalar", lambda e: e.activation(sA[:, :], sB[:, :], AF.Ln, bias=1.0), ssc)
        S.wait("vector", ssc, w_ln)
        S.op("vector", lambda e: e.tensor_scalar_max(sB[:, :], r_t[:, :], 0.0))
        S.op("vector", lambda e: e.tensor_add(r_t[:, :], sA[:, :], sB[:, :]))
        S.op("vector", lambda e: e.tensor_copy(e16[0][:, :], r_t[:, :]))
        S.op("vector", lambda e: e.tensor_scalar_mul(sA[:, :], r_t[:, :], A_col[:, 0:1]))
        for c in range(NCHUNK):
            S.op("vector", _f(lambda e, c=c: e.tensor_tensor_scan(
                dAc_l[:, CH * c:CH * (c + 1)], ones64[:, 0:CH], sA[:, CH * c:CH * (c + 1)],
                0.0, ALU.mult, ALU.add)))
        w_scan = S.op("vector", lambda e: e.tensor_tensor_scan(
            sB[:, :], ones64[:, :], sA[:, :], 0.0, ALU.mult, ALU.add), vv)
        S.wait("scalar", vv, w_scan)
        S.op("scalar", lambda e: e.activation(e16[2][:, :], dAc_l[:, :], AF.Exp))
        w_eg = S.op("scalar", lambda e: e.activation(e16[3][:, :], sB[:, :], AF.Exp), ssc)
        for c in range(NCHUNK):
            S.op("scalar", _f(lambda e, c=c: e.activation(
                ds16[:, CH * c:CH * (c + 1)], dAc_l[:, CH * c:CH * (c + 1)], AF.Exp,
                scale=-1.0, bias=dAc_l[:, CH * c + CH - 1:CH * c + CH])))
        w_ds = S.op("scalar", lambda e: e.activation(ds16[:, 0:1], ds16[:, 0:1], AF.Copy), ssc)
        S.wait("vector", ssc, w_ds)
        w_fac = S.op("vector", lambda e: e.tensor_mul(e16[1][:, :], r_t[:, :], ds16[:, :]), vv)

        S.wait("sync", vv, w_fac)
        S.dma("sync", st, dAc_st[:, :], dAc_l[:, :])
        for qd in range(4):
            S.dma("sync", st, tmq_st[NH * qd:NH * (qd + 1), :], e16[qd][:, :])
        S.wait("gpsimd", vv, w_scan)
        for c in range(NCHUNK):
            w_das = S.dma("gpsimd", gq, dasum_st[c:c + 1, :],
                          bass.AP(dAc_l, CH * c + CH - 1, [[T, NH], [1, 1]]))
        S.wait("sync", gq, w_das)
        w_darg = S.dma("sync", sq, dargs[:, :],
                       bass.AP(dasum_st, 0, [[0, 128], [1, NCHUNK * NH]]))
        S.wait("scalar", sq, w_darg)
        S.op("scalar", lambda e: e.activation(decay_reps[:, :], dargs[:, :], AF.Exp), ssc)
        S.wait("sync", st, st.v)
        for c in range(NCHUNK):
            S.dma("sync", sq, tmq_tm[:, 256 * c:256 * (c + 1)],
                  tmq_st[:, CH * c:CH * (c + 1)], transpose=True)
        S.wait("tensor", vv, w_scan)
        for c in range(NCHUNK):
            w_tr = S.op("tensor", _f(lambda e, c=c: e.transpose(
                banks[2][0:128, 0:64], dAc_l[:, CH * c:CH * (c + 1)],
                ident32[0:64, 0:64])), mm)
            S.wait("vector", mm, w_tr)
            w_cp = S.op("vector", _f(lambda e, c=c: e.tensor_copy(
                dAc_tm[:, NH * c:NH * (c + 1)], banks[2][0:128, 0:64])), vv)
            S.wait("tensor", vv, w_cp)
        if "t_dt" in taps:
            S.dma("sync", st, taps["t_dt"][:, :], r_t[:, :])
        if "t_dAc" in taps:
            S.dma("sync", st, taps["t_dAc"][:, :], dAc_l[:, :])
        S.wait("sync", st, st.v)
        S.wait("sync", sq, sq.v)

    if BUILD_UPTO >= 4:
        w_send = S.op("vector", lambda e: e.tensor_scalar_mul(
            stash_halo[:, :], stash_halo[:, :], flagA[:, 0:1]), vv)
        S.wait("gpsimd", vv, w_send)
        w_ci = S.dma("gpsimd", gq,
                     bass.AP(cc1_in, 0, [[3, 128], [3 * 128, NXBC], [1, 3]]),
                     bass.AP(stash_halo, 0, [[NXBC * 3, 128], [3, NXBC], [1, 3]]))
        S.wait("gpsimd", gq, w_ci)
        ccs.v += 1
        S.q["gpsimd"].append(lambda g: g.collective_compute(
            "AllReduce", ALU.add, replica_groups=[[0, 1], [2, 3], [4, 5], [6, 7]],
            ins=[cc1_in.ap().opt()], outs=[cc1_out.ap().opt()]).then_inc(ccs.h, 1))
        S.wait("gpsimd", ccs, ccs.v)
        w_co = S.dma("gpsimd", gq,
                     bass.AP(stash_halo, 0, [[NXBC * 3, 128], [3, NXBC], [1, 3]]),
                     bass.AP(cc1_out, 0, [[3, 128], [3 * 128, NXBC], [1, 3]]))
        S.wait("vector", gq, w_co)
        S.op("vector", lambda e: e.tensor_scalar_mul(
            stash_halo[:, :], stash_halo[:, :], flagB[:, 0:1]))
        p3s = ExitStack()
        six_all = p3s.enter_context(nc.sbuf_tensor("six_all", [128, NXBC * 6], F32))
        c3a = p3s.enter_context(nc.sbuf_tensor("c3a", [128, NXBC * 3], F32))
        S.op("vector", lambda e: e.tensor_copy(
            six_all[:, :].rearrange("p (f c) -> p f c", c=6)[:, :, 0:3],
            stash_halo[:, :].rearrange("p (f c) -> p f c", c=3)))
        w_pk = S.op("vector", lambda e: e.tensor_copy(
            six_all[:, :].rearrange("p (f c) -> p f c", c=6)[:, :, 3:6],
            stash_first[:, :].rearrange("p (f c) -> p f c", c=3)), vv)
        S.wait("vector", vv, w_pk)

        def _win(i):
            return six_all[:, :].rearrange("p (f c) -> p f c", c=6)[:, :, i:i + 3]

        def _cw(i):
            return bass.AP(cw_cols, i, [[NXBC * 4, 128], [4, NXBC], [0, 3]])

        cb_b = bass.AP(cb_cols, 0, [[NXBC, 128], [1, NXBC], [0, 3]])
        w_t0 = S.op("vector", lambda e: e.tensor_tensor(
            fixpack[:, :].rearrange("p (f c) -> p f c", c=3), _win(0), _cw(0), ALU.mult), vv)
        S.wait("vector", vv, w_t0)
        w_t0b = S.op("vector", lambda e: e.tensor_tensor(
            fixpack[:, :].rearrange("p (f c) -> p f c", c=3),
            fixpack[:, :].rearrange("p (f c) -> p f c", c=3), cb_b, ALU.add), vv)
        S.wait("vector", vv, w_t0b)
        for i in (1, 2, 3):
            w_ta = S.op("vector", _f(lambda e, i=i: e.tensor_tensor(
                c3a[:, :].rearrange("p (f c) -> p f c", c=3), _win(i), _cw(i), ALU.mult)), vv)
            S.wait("vector", vv, w_ta)
            w_tb = S.op("vector", _f(lambda e: e.tensor_add(
                fixpack[:, :], fixpack[:, :], c3a[:, :])), vv)
            S.wait("vector", vv, w_tb)
        S.wait("scalar", vv, vv.v)
        w_silu3 = S.op("scalar", lambda e: e.activation(fixp16[:, :], fixpack[:, :], AF.Silu), ssc)
        S.wait("sync", ssc, w_silu3)
        S.dma("sync", st, bass.AP(act_st, 0, [[T, 128], [128 * T, NXBC], [1, 3]]),
              bass.AP(fixp16, 0, [[NXBC * 3, 128], [3, NXBC], [1, 3]]))
        S.wait("vector", ssc, w_silu3)
        S.op("vector", lambda e: e.tensor_copy(BC_fm[:, 0:3], fixp16[:, 96:99]))
        S.op("vector", lambda e: e.tensor_copy(BC_fm[:, T:T + 3], fixp16[:, 99:102]), ctr=None)
        S.wait("sync", st, st.v)
        S.emit("p2all")
        p3s.close()
        p2.close()
        pA.close()
        pdt2.close()

        if "t_act" in taps:
            p3t = ExitStack()
            atap = p3t.enter_context(nc.sbuf_tensor("atap", [128, T], F32))
            for fc in range(NXBC):
                S.wait("gpsimd", st, st.v)
                w_l = S.dma("gpsimd", gq, atap[:, :], act_st[128 * fc:128 * (fc + 1), :])
                S.wait("sync", gq, w_l)
                S.dma("sync", st, taps["t_act"][128 * fc:128 * (fc + 1), :], atap[:, :])
            S.wait("sync", st, st.v)
            S.emit("tapa")
            p3t.close()

    pB2 = ExitStack()
    y_tm = pB2.enter_context(nc.sbuf_tensor("y_tm", [128, NCHUNK * DIN], F16))
    S_f32 = pB2.enter_context(nc.sbuf_tensor("S_f32", [128, DIN], F32))
    S_f16 = pB2.enter_context(nc.sbuf_tensor("S_f16", [128, DIN], F16))
    SSt = pB2.enter_context(nc.sbuf_tensor("SSt", [128, DIN], F16))
    nw_rep = pB2.enter_context(nc.sbuf_tensor("nw_rep", [128, DIN], F16))

    if BUILD_UPTO >= 5:
        p4 = ExitStack()
        xb = [p4.enter_context(nc.sbuf_tensor(f"xb{i}", [128, CONVD], F16)) for i in range(2)]
        rq = [p4.enter_context(nc.sbuf_tensor(f"rq{i}", [128, 16 * CH], F32)) for i in range(2)]
        larg = p4.enter_context(nc.sbuf_tensor("larg", [128, 16 * CH], F32))
        me16 = p4.enter_context(nc.sbuf_tensor("me16", [128, 16 * CH], F16))
        mh16 = [p4.enter_context(nc.sbuf_tensor(f"mh16{i}", [128, 16 * CH], F16)) for i in range(2)]
        gt_sb = [p4.enter_context(nc.sbuf_tensor(f"gt_sb{i}", [128, 128], F16)) for i in range(2)]
        xsc = [p4.enter_context(nc.sbuf_tensor(f"xsc{i}", [128, 512], F16)) for i in range(2)]
        xst_ = [p4.enter_context(nc.sbuf_tensor(f"xst{i}", [128, 512], F16)) for i in range(2)]
        tmpv = [p4.enter_context(nc.sbuf_tensor(f"tmpv{i}", [128, 512], F32)) for i in range(2)]
        skp = [p4.enter_context(nc.sbuf_tensor(f"skp{i}", [128, 512], F16)) for i in range(2)]

        S.wait("sync", ast[0], ast[0].v)
        S.wait("sync", ast[1], ast[1].v)
        S.wait("sync", st, st.v)
        S.op("gpsimd", lambda e: e.memset(S_f32[:, :], 0.0))
        w_s0 = S.op("gpsimd", lambda e: e.memset(S_f16[:, :], 0.0), pq)
        S.wait("tensor", pq, w_s0)
        S.wait("vector", pq, w_s0)

        xb_ready = [0] * NCHUNK
        rq_ready = {}
        xb_freed = {}
        rq_freed = {}
        rq_freed_ctr = {}
        mh_pool = {}
        mh_freed = {}
        xsc_freed = {}
        ev_freed = {}

        xb_ready[0] = S.dma("sync", xq[0], xb[0][:, :], act_st[:, 0:CH], transpose=True)
        rq_ready[(0, 0)] = S.dma("sync", rqs[0], rq[0][:, :],
                                 bass.AP(dAc_st, 0, [[0, 128], [T, 16], [1, CH]]))

        yd_bank = {0: banks[2], 1: banks[4]}
        yo_bank = {0: banks[3], 1: banks[5]}
        st_bank = {0: banks[6], 1: banks[7]}

        last_dve_group = {}
        for c in range(NCHUNK):
            S.wait("tensor", xq[c % 2], xb_ready[c])
            if c >= 1:
                S.wait("tensor", vv, gt_evac)
            if c + 1 < NCHUNK:
                if c >= 1:
                    S.wait("sync", mm, xb_freed.get(c - 1, 0))
                xb_ready[c + 1] = S.dma("sync", xq[(c + 1) % 2], xb[(c + 1) % 2][:, :],
                                        act_st[:, CH * (c + 1):CH * (c + 2)], transpose=True)
            w_gt = S.op("tensor", _f(lambda e, c=c: e.matmul(
                banks[0][0:128, 0:128], BC_fm[:, CH * c:CH * (c + 1)],
                BC_fm[:, T + CH * c:T + CH * (c + 1)], start=True, stop=True)), mm)
            S.wait("vector", mm, w_gt)
            gt_evac = S.op("vector", _f(lambda e, c=c: e.tensor_tensor(
                gt_sb[c % 2][:, :], banks[0][0:128, 0:128], tril01[:, :], ALU.mult)), vv)

            for q in range(4):
                S.wait("vector", rqs[(c * 4 + q) % 2], rq_ready[(c, q)])

                nqc, nqq = (c, q + 1) if q < 3 else (c + 1, 0)
                if nqc < NCHUNK:
                    pos = nqc * 4 + nqq
                    if pos >= 2:
                        ppc, ppq = divmod(pos - 2, 4)
                        S.wait("sync", rq_freed_ctr.get((ppc, ppq), vv),
                               rq_freed.get((ppc, ppq), 0))
                    rq_ready[(nqc, nqq)] = S.dma(
                        "sync", rqs[pos % 2], rq[pos % 2][:, :],
                        bass.AP(dAc_st, 16 * T * nqq + CH * nqc,
                                [[0, 128], [T, 16], [1, CH]]))
                rbuf = rq[(c * 4 + q) % 2]
                w_lg = S.op("vector", _f(lambda e, c=c, q=q, rbuf=rbuf: e.tensor_tensor(
                    larg[:, :].rearrange("p (h t) -> p h t", h=16),
                    rbuf[:, :].rearrange("p (h t) -> p h t", h=16),
                    _bcast_hp(dAc_tm[:, NH * c + 16 * q:NH * c + 16 * (q + 1)], 16, CH),
                    ALU.subtract)), vv)
                S.wait("vector", vv, w_lg)
                w_cl = S.op("vector", _f(lambda e, rbuf=rbuf: e.tensor_scalar_min(
                    rbuf[:, :], larg[:, :], 0.0)), vv)
                rq_freed[(c, q)] = w_cl
                rq_freed_ctr[(c, q)] = vv
                S.wait("scalar", vv, w_cl)
                if c * 4 + q >= 1:
                    mpc, mpq = divmod(c * 4 + q - 1, 4)
                    S.wait("scalar", vv, mh_pool.get((mpc, mpq), 0))
                w_me = S.op("scalar", _f(lambda e, rbuf=rbuf: e.activation(
                    me16[:, :], rbuf[:, :], AF.Exp)), ssc)
                S.wait("vector", ssc, w_me)
                S.wait("vector", vv, gt_evac)
                if c * 4 + q >= 2:
                    pc, pqq = divmod(c * 4 + q - 2, 4)
                    S.wait("vector", mm, mh_freed.get((pc, pqq), 0))
                w_mh = S.op("vector", _f(lambda e, c=c, q=q: e.tensor_tensor(
                    mh16[q % 2][:, :].rearrange("p (h t) -> p h t", h=16),
                    me16[:, :].rearrange("p (h t) -> p h t", h=16),
                    _bcast_outer(gt_sb[c % 2][:, :], 16), ALU.mult)), vv)
                mh_pool[(c, q)] = w_mh

                for gl in range(2):
                    g = 2 * q + gl
                    par = g % 2
                    if c * 8 + g >= 2:
                        S.wait("vector", mm, xsc_freed.get((c * 8 + g - 2), 0))
                    S.op("vector", _f(lambda e, c=c, g=g, par=par: e.tensor_tensor(
                        xsc[par][:, :].rearrange("p (h d) -> p h d", h=8),
                        xb[c % 2][:, 512 * g:512 * (g + 1)].rearrange("p (h d) -> p h d", h=8),
                        _bcast_hp(tmq_tm[:, 256 * c + 8 * g:256 * c + 8 * (g + 1)], 8, HPD),
                        ALU.mult)))
                    w_xs = S.op("vector", _f(lambda e, c=c, g=g, par=par: e.tensor_tensor(
                        xst_[par][:, :].rearrange("p (h d) -> p h d", h=8),
                        xb[c % 2][:, 512 * g:512 * (g + 1)].rearrange("p (h d) -> p h d", h=8),
                        _bcast_hp(tmq_tm[:, 256 * c + 64 + 8 * g:256 * c + 64 + 8 * (g + 1)], 8, HPD),
                        ALU.mult)), vv)
                    S.wait("tensor", vv, w_mh)
                    S.wait("tensor", vv, w_xs)
                    S.wait("tensor", vv, ev_freed.get(("yd", par, c, g), 0))
                    for hl in range(8):
                        h = 8 * g + hl
                        hq = h - 16 * q
                        def ydfn(e, c=c, q=q, par=par, hl=hl, hq=hq, h=h):
                            return e.matmul(
                                yd_bank[par][0:128, 64 * hl:64 * (hl + 1)],
                                mh16[q % 2][:, 128 * hq:128 * (hq + 1)],
                                xsc[par][:, 64 * hl:64 * (hl + 1)],
                                start=(hl == 0), stop=(hl == 7))
                        if hl == 7:
                            w_yd = S.op("tensor", ydfn, mm)
                        else:
                            S.op("tensor", ydfn)
                    S.wait("tensor", vv, ev_freed.get(("yo", par, c, g), 0))
                    w_yo = S.op("tensor", _f(lambda e, c=c, g=g, par=par: e.matmul(
                        yo_bank[par][0:128, :], BC_fm[:, T + CH * c:T + CH * (c + 1)],
                        S_f16[:, 512 * g:512 * (g + 1)], start=True, stop=True)), mm)
                    S.wait("tensor", vv, ev_freed.get(("st", par, c, g), 0))
                    w_st = S.op("tensor", _f(lambda e, c=c, g=g, par=par: e.matmul(
                        st_bank[par][0:128, :], xb[c % 2][:, DIN:DIN + 128],
                        xst_[par][:, :], start=True, stop=True)), mm)
                    mh_freed[(c, q)] = w_yd if gl == 1 else mh_freed.get((c, q), 0)
                    xsc_freed[c * 8 + g] = w_st
                    if g == 7:
                        xb_freed[c] = w_st

                    S.wait("vector", mm, w_yo)
                    S.op("vector", _f(lambda e, c=c, g=g, par=par: e.tensor_tensor(
                        tmpv[par][:, :].rearrange("p (h d) -> p h d", h=8),
                        yo_bank[par][0:128, :].rearrange("p (h d) -> p h d", h=8),
                        _bcast_hp(tmq_tm[:, 256 * c + 128 + 8 * g:256 * c + 128 + 8 * (g + 1)], 8, HPD),
                        ALU.mult)))
                    S.wait("vector", mm, w_yd)
                    ev1 = S.op("vector", _f(lambda e, c=c, g=g, par=par: e.tensor_tensor(
                        y_tm[:, DIN * c + 512 * g:DIN * c + 512 * (g + 1)],
                        tmpv[par][:, :], yd_bank[par][0:128, :], ALU.add)), vv)
                    ev_freed[("yd", par, c + (1 if g >= 6 else 0), (g + 2) % 8)] = ev1
                    ev_freed[("yo", par, c + (1 if g >= 6 else 0), (g + 2) % 8)] = ev1
                    S.op("vector", _f(lambda e, c=c, g=g, par=par: e.tensor_tensor(
                        skp[par][:, :].rearrange("p (h d) -> p h d", h=8),
                        xb[c % 2][:, 512 * g:512 * (g + 1)].rearrange("p (h d) -> p h d", h=8),
                        _bcast_hp(D_rep[:, 8 * g:8 * (g + 1)], 8, HPD), ALU.mult)))
                    S.op("vector", _f(lambda e, c=c, g=g, par=par: e.tensor_tensor(
                        y_tm[:, DIN * c + 512 * g:DIN * c + 512 * (g + 1)],
                        y_tm[:, DIN * c + 512 * g:DIN * c + 512 * (g + 1)],
                        skp[par][:, :], ALU.add)))
                    S.wait("vector", mm, w_st)
                    S.op("vector", _f(lambda e, c=c, g=g, par=par: e.tensor_tensor(
                        tmpv[par][:, :].rearrange("p (h d) -> p h d", h=8),
                        S_f32[:, 512 * g:512 * (g + 1)].rearrange("p (h d) -> p h d", h=8),
                        _bcast_hp(decay_reps[:, NH * c + 8 * g:NH * c + 8 * (g + 1)], 8, HPD),
                        ALU.mult)))
                    S.op("vector", _f(lambda e, c=c, g=g, par=par: e.tensor_add(
                        S_f32[:, 512 * g:512 * (g + 1)], tmpv[par][:, :],
                        st_bank[par][0:128, :])))
                    ev2 = S.op("vector", _f(lambda e, c=c, g=g: e.tensor_copy(
                        S_f16[:, 512 * g:512 * (g + 1)], S_f32[:, 512 * g:512 * (g + 1)])), vv)
                    ev_freed[("st", par, c + (1 if g >= 6 else 0), (g + 2) % 8)] = ev2
                    last_dve_group[(c, g)] = ev2
        S.emit("p4")
        p4.close()

        if "t_S" in taps:
            with nc.sbuf_tensor("stap", [128, DIN], F32) as stap:
                S.op("vector", lambda e: e.tensor_copy(stap[:, :], S_f32[:, :]), vv)
                S.wait("sync", vv, vv.v)
                S.dma("sync", st, taps["t_S"][:, :], stap[:, :])
                S.wait("sync", st, st.v)
                S.emit("taps4")

    if BUILD_UPTO >= 6:
        p5 = ExitStack()
        ar_scr = p5.enter_context(nc.sbuf_tensor("ar_scr", [128, DIN], F32))
        w_snd = S.op("vector", lambda e: e.tensor_scalar_mul(
            ar_scr[:, :], S_f32[:, :], flagA[:, 0:1]), vv)
        S.wait("gpsimd", vv, w_snd)
        w_ci2 = S.dma("gpsimd", gq, cc2_in[:, :], ar_scr[:, :])
        S.wait("gpsimd", gq, w_ci2)
        ccs.v += 1
        S.q["gpsimd"].append(lambda g: g.collective_compute(
            "AllReduce", ALU.add, replica_groups=[[0, 1], [2, 3], [4, 5], [6, 7]],
            ins=[cc2_in.ap().opt()], outs=[cc2_out.ap().opt()]).then_inc(ccs.h, 1))
        S.wait("gpsimd", ccs, ccs.v)
        w_co2 = S.dma("gpsimd", gq, ar_scr[:, :], cc2_out[:, :])
        S.wait("vector", gq, w_co2)
        w_ss = S.op("vector", lambda e: e.tensor_scalar_mul(
            SSt[:, :], ar_scr[:, :], flagB[:, 0:1]), vv)
        S.emit("p5")
        p5.close()

        p6 = ExitStack()
        tmpe = p6.enter_context(nc.sbuf_tensor("tmpe", [128, DIN], F32))
        w_nw = S.dma("gpsimd", gq, nw_rep[:, :], bass.AP(norm_w, 0, [[0, 128], [1, DIN]]))
        p7 = ExitStack()
        gtm = [p7.enter_context(nc.sbuf_tensor(f"gtm{i}", [128, DIN], F16)) for i in range(2)]
        sgt = p7.enter_context(nc.sbuf_tensor("sgt", [128, DIN], F16))
        hgt = p7.enter_context(nc.sbuf_tensor("hgt", [128, DIN], F16))
        cl7 = p7.enter_context(nc.sbuf_tensor("cl7", [128, 4], F32))
        S.wait("sync", gst[0], gst[0].v)
        S.wait("sync", gst[1], gst[1].v)
        g_ready = [0] * NCHUNK
        g_ready[0] = S.dma("sync", xq[0], gtm[0][:, :], gate_st[:, 0:CH], transpose=True)
        yst = {}
        S.wait("vector", gq, w_nw)
        sg_hist = {}
        ep_ev = {}
        for c in range(NCHUNK):
            for g in range(8):
                par = g % 2
                S.wait("tensor", vv, ep_ev.get((c, g), 0))
                w_yx = S.op("tensor", _f(lambda e, c=c, g=g, par=par: e.matmul(
                    yd_bank[par][0:128, :], BC_fm[:, T + CH * c:T + CH * (c + 1)],
                    SSt[:, 512 * g:512 * (g + 1)], start=True, stop=True)), mm)
                S.wait("vector", mm, w_yx)
                w_sc = S.op("vector", _f(lambda e, c=c, g=g, par=par: e.tensor_tensor(
                    tmpe[:, 512 * g:512 * (g + 1)].rearrange("p (h d) -> p h d", h=8),
                    yd_bank[par][0:128, :].rearrange("p (h d) -> p h d", h=8),
                    _bcast_hp(tmq_tm[:, 256 * c + 192 + 8 * g:256 * c + 192 + 8 * (g + 1)], 8, HPD),
                    ALU.mult)), vv)
                ep_ev[(c, g + 2) if g < 6 else (c + 1, (g + 2) % 8)] = w_sc
            w_ad = S.op("vector", _f(lambda e, c=c: e.tensor_add(
                y_tm[:, DIN * c:DIN * (c + 1)], y_tm[:, DIN * c:DIN * (c + 1)],
                tmpe[:, :])), vv)
            w_ssu = S.op("vector", _f(lambda e, c=c: e.tensor_tensor(
                SSt[:, :].rearrange("p (h d) -> p h d", h=NH),
                SSt[:, :].rearrange("p (h d) -> p h d", h=NH),
                _bcast_hp(decay_reps[:, NH * c:NH * (c + 1)], NH, HPD), ALU.mult)), vv)
            S.wait("tensor", vv, w_ssu)
            S.wait("scalar", xq[c % 2], g_ready[c])
            if c + 1 < NCHUNK:
                if c >= 1:
                    S.wait("sync", ssc, sg_hist.get(c - 1, 0))
                g_ready[c + 1] = S.dma("sync", xq[(c + 1) % 2], gtm[(c + 1) % 2][:, :],
                                       gate_st[:, CH * (c + 1):CH * (c + 2)], transpose=True)
            S.wait("scalar", gst[(c - 1) % 2], yst.get(c - 1, 0))
            w_sg = S.op("scalar", _f(lambda e, c=c: e.activation(
                sgt[:, :], gtm[c % 2][:, :], AF.Silu)), ssc)
            sg_hist[c] = w_sg
            S.wait("vector", ssc, w_sg)
            w_hg = S.op("vector", _f(lambda e, c=c: e.tensor_tensor(
                hgt[:, :], y_tm[:, DIN * c:DIN * (c + 1)], sgt[:, :], ALU.mult)), vv)
            S.wait("scalar", vv, w_hg)
            w_sq2 = S.op("scalar", _f(lambda e: e.activation(
                sgt[:, :], hgt[:, :], AF.Square, accum_out=cl7[:, 0:1])), ssc)
            S.wait("vector", ssc, w_sq2)
            S.op("vector", _f(lambda e: e.tensor_single_scalar(
                cl7[:, 1:2], cl7[:, 0:1], 1.0 / DIN, ALU.mult)), vv)
            S.wait("scalar", vv, vv.v)
            w_sq3 = S.op("scalar", _f(lambda e: e.activation(
                cl7[:, 2:3], cl7[:, 1:2], AF.Sqrt, bias=eps_col[:, 0:1])), ssc)
            S.wait("vector", ssc, w_sq3)
            w_rc7 = S.op("vector", _f(lambda e: e.reciprocal(cl7[:, 3:4], cl7[:, 2:3])), vv)
            S.wait("vector", vv, w_rc7)
            S.op("vector", _f(lambda e: e.tensor_scalar_mul(sgt[:, :], hgt[:, :], cl7[:, 3:4])))
            w_yn = S.op("vector", _f(lambda e: e.tensor_mul(hgt[:, :], sgt[:, :], nw_rep[:, :])), vv)
            S.wait("sync", vv, w_yn)
            yst[c] = S.dma("sync", gst[c % 2], yn_st[CH * c:CH * (c + 1), :], hgt[:, :])
        S.wait("sync", gst[0], gst[0].v)
        S.wait("sync", gst[1], gst[1].v)
        S.emit("p67")
        p7.close()
        p6.close()

        if "t_y" in taps:
            with nc.sbuf_tensor("ytp", [128, DIN], F32) as ytp:
                for c in range(NCHUNK):
                    S.wait("vector", st, st.v)
                    S.op("vector", _f(lambda e, c=c: e.tensor_copy(
                        ytp[:, :], y_tm[:, DIN * c:DIN * (c + 1)])), vv)
                    S.wait("sync", vv, vv.v)
                    S.dma("sync", st, taps["t_y"][CH * c:CH * (c + 1), :], ytp[:, :])
                S.wait("sync", st, st.v)
                S.emit("tapy")

    if BUILD_UPTO >= 7:
        pass
        emit_dram_tap("t_yn", yn_st, T, DIN)
    pB2.close()
    pB1.close()

    if BUILD_UPTO >= 8:
        p8 = ExitStack()
        ynfm = p8.enter_context(nc.sbuf_tensor("ynfm", [128, 32 * T], F16))
        wo_buf = p8.enter_context(nc.sbuf_tensor("wo_buf", [128, 2 * 32 * 512], F16))
        astg8 = [p8.enter_context(nc.sbuf_tensor(f"astg8_{i}", [128, T], F16)) for i in range(2)]
        for k in range(32):
            w_yl = S.dma("sync", bq, ynfm[:, T * k:T * (k + 1)],
                         yn_st[:, 128 * k:128 * (k + 1)], transpose=True)
        WHALF = 32 * 512
        ev8 = {}
        sto8 = {}
        slab_mm8 = []
        for fb in range(4):
            half = fb % 2
            if fb >= 2:
                S.wait("gpsimd", mm, slab_mm8[fb - 2])
            for k0 in (0, 8, 16, 24):
                w_sl = S.dma("gpsimd", gqh[half],
                             bass.AP(wo_buf, half * WHALF + k0 * 512,
                                     [[2 * WHALF, 128], [512, 8], [1, 512]]),
                             bass.AP(w_out_T, k0 * 128 * DM + 512 * fb,
                                     [[DM, 128], [DM * 128, 8], [1, 512]]))
            S.wait("tensor", gqh[half], w_sl)
            if fb == 0:
                S.wait("tensor", bq, w_yl)
            for fc in range(4):
                f_g = 4 * fb + fc
                for tt in range(2):
                    pi = (f_g * 2 + tt) % 2
                    S.wait("tensor", ssc, ev8.get((f_g * 2 + tt) - 2, 0))
                    for k in range(32):
                        def mo(e, half=half, k=k, fc=fc, tt=tt, pi=pi):
                            return e.matmul(
                                banks[pi][0:128, :],
                                bass.AP(wo_buf, half * WHALF + k * 512 + fc * 128,
                                        [[2 * WHALF, 128], [1, 128]]),
                                ynfm[:, T * k + 512 * tt:T * k + 512 * tt + 512],
                                start=(k == 0), stop=(k == 31))
                        if k == 31:
                            w_m8 = S.op("tensor", mo, mm)
                        else:
                            S.op("tensor", mo)
                    S.wait("scalar", mm, w_m8)
                    if tt == 0:
                        S.wait("scalar", ast[f_g % 2], sto8.get(f_g - 2, 0))
                    ev8[f_g * 2 + tt] = S.op("scalar", _f(
                        lambda e, f_g=f_g, tt=tt, pi=pi: e.activation(
                            astg8[f_g % 2][:, 512 * tt:512 * (tt + 1)],
                            banks[pi][0:128, :], AF.Copy)), ssc)
                    if tt == 1:
                        S.wait("sync", ssc, ev8[f_g * 2 + 1])
                        sto8[f_g] = S.dma("sync", ast[f_g % 2],
                                          attn_st[128 * f_g:128 * (f_g + 1), :],
                                          astg8[f_g % 2][:, :])
            slab_mm8.append(mm.v)
        S.wait("sync", ast[0], ast[0].v)
        S.wait("sync", ast[1], ast[1].v)
        S.emit("p8")
        p8.close()
        emit_dram_tap("t_attn", attn_st, DM, T)

    pw0 = ExitStack()
    if BUILD_UPTO >= 10:
        w0_buf = pw0.enter_context(nc.sbuf_tensor("w0_buf", [128, 2 * KD * 512], F16))
        H0 = KD * 512
    if BUILD_UPTO >= 9:
        p9 = ExitStack()
        atm = [p9.enter_context(nc.sbuf_tensor(f"atm{i}", [128, DM], F16)) for i in range(2)]
        xr = [p9.enter_context(nc.sbuf_tensor(f"xr{i}", [128, DM], F32)) for i in range(2)]
        nrm = p9.enter_context(nc.sbuf_tensor("nrm", [128, DM], F32))
        x2t = p9.enter_context(nc.sbuf_tensor("x2t", [128, DM], F32))
        x2n16 = p9.enter_context(nc.sbuf_tensor("x2n16", [128, DM], F16))
        scr9 = p9.enter_context(nc.sbuf_tensor("scr9", [128, DM], F16))
        cl9 = p9.enter_context(nc.sbuf_tensor("cl9", [128, 8], F32))
        st9 = {}
        dve_done9 = {}
        for i in range(8):
            if i >= 2:
                S.wait("sync", vv, dve_done9.get(i - 2, 0))
            w_a = S.dma("sync", xq[i % 2], atm[i % 2][:, :],
                        attn_st[:, CH * i:CH * (i + 1)], transpose=True)
            w_x = S.dma("sync", xq[i % 2], xr[i % 2][:, :], x_in[CH * i:CH * (i + 1), :])
            S.wait("scalar", xq[i % 2], w_a)
            S.wait("scalar", ast[i % 2], st9.get(i - 2, 0))
            w_s1 = S.op("scalar", _f(lambda e, i=i: e.activation(
                scr9[:, :], atm[i % 2][:, :], AF.Square, accum_out=cl9[:, 1:2])), ssc)
            S.wait("vector", xq[i % 2], w_a)
            w_rs9 = S.op("vector", _f(lambda e, i=i: e.reduce_sum(cl9[:, 0:1], atm[i % 2][:, :], axis=AX.X)), vv)
            S.wait("vector", vv, w_rs9)
            S.op("vector", _f(lambda e: e.tensor_single_scalar(cl9[:, 2:3], cl9[:, 0:1], 1.0 / DM, ALU.mult)))
            S.wait("vector", ssc, w_s1)
            S.op("vector", _f(lambda e: e.tensor_single_scalar(cl9[:, 3:4], cl9[:, 1:2], 1.0 / DM, ALU.mult)))
            w_m29 = S.op("vector", _f(lambda e: e.tensor_tensor(cl9[:, 4:5], cl9[:, 2:3], cl9[:, 2:3], ALU.mult)), vv)
            S.wait("vector", vv, w_m29)
            w_v9 = S.op("vector", _f(lambda e: e.tensor_tensor(cl9[:, 4:5], cl9[:, 3:4], cl9[:, 4:5], ALU.subtract)), vv)
            S.wait("scalar", vv, w_v9)
            w_sq9 = S.op("scalar", _f(lambda e: e.activation(cl9[:, 5:6], cl9[:, 4:5], AF.Sqrt, bias=eps_col[:, 0:1])), ssc)
            S.wait("vector", ssc, w_sq9)
            w_rc9 = S.op("vector", _f(lambda e: e.reciprocal(cl9[:, 6:7], cl9[:, 5:6])), vv)
            S.wait("vector", vv, w_rc9)
            S.op("vector", _f(lambda e, i=i: e.tensor_scalar(
                nrm[:, :], atm[i % 2][:, :], cl9[:, 2:3], cl9[:, 6:7], ALU.subtract, ALU.mult)))
            S.wait("vector", xq[i % 2], w_x)
            S.wait("vector", x2q, x2q.v)
            w_x2 = S.op("vector", _f(lambda e, i=i: e.tensor_add(
                x2t[:, :], xr[i % 2][:, :], nrm[:, :])), vv)
            S.wait("sync", vv, w_x2)
            S.dma("sync", x2q, x2_st[CH * i:CH * (i + 1), :], x2t[:, :])
            S.wait("scalar", vv, w_x2)
            w_s2 = S.op("scalar", _f(lambda e: e.activation(
                scr9[:, :], x2t[:, :], AF.Square, accum_out=cl9[:, 1:2])), ssc)
            w_rs92 = S.op("vector", _f(lambda e: e.reduce_sum(cl9[:, 0:1], x2t[:, :], axis=AX.X)), vv)
            S.wait("vector", vv, w_rs92)
            S.op("vector", _f(lambda e: e.tensor_single_scalar(cl9[:, 2:3], cl9[:, 0:1], 1.0 / DM, ALU.mult)))
            S.wait("vector", ssc, w_s2)
            S.op("vector", _f(lambda e: e.tensor_single_scalar(cl9[:, 3:4], cl9[:, 1:2], 1.0 / DM, ALU.mult)))
            w_m292 = S.op("vector", _f(lambda e: e.tensor_tensor(cl9[:, 4:5], cl9[:, 2:3], cl9[:, 2:3], ALU.mult)), vv)
            S.wait("vector", vv, w_m292)
            w_v92 = S.op("vector", _f(lambda e: e.tensor_tensor(cl9[:, 4:5], cl9[:, 3:4], cl9[:, 4:5], ALU.subtract)), vv)
            S.wait("scalar", vv, w_v92)
            w_sq92 = S.op("scalar", _f(lambda e: e.activation(cl9[:, 5:6], cl9[:, 4:5], AF.Sqrt, bias=eps_col[:, 0:1])), ssc)
            S.wait("vector", ssc, w_sq92)
            w_rc92 = S.op("vector", _f(lambda e: e.reciprocal(cl9[:, 6:7], cl9[:, 5:6])), vv)
            S.wait("vector", vv, w_rc92)
            S.wait("vector", ast[(i - 1) % 2], st9.get(i - 1, 0))
            w_n2 = S.op("vector", _f(lambda e: e.tensor_scalar(
                x2n16[:, :], x2t[:, :], cl9[:, 2:3], cl9[:, 6:7], ALU.subtract, ALU.mult)), vv)
            S.wait("sync", vv, w_n2)
            dve_done9[i] = w_n2
            st9[i] = S.dma("sync", ast[i % 2], x2n_st[CH * i:CH * (i + 1), :], x2n16[:, :])
        S.wait("sync", ast[0], ast[0].v)
        S.wait("sync", ast[1], ast[1].v)
        S.wait("sync", x2q, x2q.v)
        w0_pre = []
        if BUILD_UPTO >= 10:
            for sb in range(2):
                for k0 in (0, 8):
                    w_p = S.dma("gpsimd", gqh[sb % 2],
                                bass.AP(w0_buf, (sb % 2) * H0 + k0 * 512,
                                        [[2 * H0, 128], [512, 8], [1, 512]]),
                                bass.AP(w0_T, k0 * 128 * 8192 + 512 * sb,
                                        [[8192, 128], [8192 * 128, 8], [1, 512]]))
                w0_pre.append(w_p)
        S.emit("p9")
        p9.close()
        emit_dram_tap("t_x2", x2_st, T, DM, f16=False)

    if BUILD_UPTO >= 10:
        p10 = ExitStack()
        d_fm = p10.enter_context(nc.sbuf_tensor("d_fm", [128, 64 * T], F16))
        p10w = ExitStack()
        x2nfm = p10w.enter_context(nc.sbuf_tensor("x2nfm", [128, KD * T], F16))
        for k in range(KD):
            w_xl = S.dma("sync", bq, x2nfm[:, T * k:T * (k + 1)],
                         x2n_st[:, 128 * k:128 * (k + 1)], transpose=True)
        slab_mm0 = []
        ev10 = {}
        for sb in range(16):
            half = sb % 2
            if sb >= 2:
                S.wait("gpsimd", mm, slab_mm0[sb - 2])
                for k0 in (0, 8):
                    w_sl0 = S.dma("gpsimd", gqh[half],
                                  bass.AP(w0_buf, half * H0 + k0 * 512,
                                          [[2 * H0, 128], [512, 8], [1, 512]]),
                                  bass.AP(w0_T, k0 * 128 * 8192 + 512 * sb,
                                          [[8192, 128], [8192 * 128, 8], [1, 512]]))
            else:
                w_sl0 = w0_pre[sb]
            S.wait("tensor", gqh[half], w_sl0)
            if sb == 0:
                S.wait("tensor", bq, w_xl)
            for ncc in range(4):
                n_g = 4 * sb + ncc
                npar = n_g % 2
                for tt in range(2):
                    S.wait("tensor", ssc, ev10.get((n_g - 2) * 2 + tt, 0))
                for k in range(KD):
                    for tt in range(2):
                        def m0(e, half=half, k=k, ncc=ncc, tt=tt, npar=npar):
                            return e.matmul(
                                banks[2 * npar + tt][0:128, :],
                                bass.AP(w0_buf, half * H0 + k * 512 + ncc * 128,
                                        [[2 * H0, 128], [1, 128]]),
                                x2nfm[:, T * k + 512 * tt:T * k + 512 * tt + 512],
                                start=(k == 0), stop=(k == KD - 1))
                        if k == KD - 1:
                            w_m0 = S.op("tensor", m0, mm)
                        else:
                            S.op("tensor", m0)
                S.wait("scalar", mm, w_m0)
                for tt in range(2):
                    ev10[n_g * 2 + tt] = S.op("scalar", _f(
                        lambda e, n_g=n_g, tt=tt, npar=npar: e.activation(
                            d_fm[:, T * n_g + 512 * tt:T * n_g + 512 * tt + 512],
                            banks[2 * npar + tt][0:128, :], AF.Gelu,
                            bias=b0_cols[:, n_g:n_g + 1])), ssc)
            slab_mm0.append(mm.v)
        S.emit("p10")
        p10w.close()

        p11 = ExitStack()
        w1_buf = p11.enter_context(nc.sbuf_tensor("w1_buf", [128, 2 * 8 * 512], F16))
        estg = [p11.enter_context(nc.sbuf_tensor(f"estg{i}", [128, T], F16)) for i in range(2)]
        H1 = 8 * 512
        d_last = ev10[127]
        sto11 = {}
        prev_pass_mm = 0
        for ps in range(4):
            slab_mm1 = []
            for kg in range(8):
                half = kg % 2
                if kg >= 2:
                    S.wait("gpsimd", mm, slab_mm1[kg - 2])
                elif ps >= 1:
                    S.wait("gpsimd", mm, prev_pass_mm)
                w_sl1 = S.dma("gpsimd", gqh[half],
                              bass.AP(w1_buf, half * H1,
                                      [[2 * H1, 128], [512, 8], [1, 512]]),
                              bass.AP(w1_T, (8 * kg) * 128 * DM + 512 * ps,
                                      [[DM, 128], [DM * 128, 8], [1, 512]]))
                S.wait("tensor", gqh[half], w_sl1)
                if ps == 0 and kg == 0:
                    S.wait("tensor", ssc, d_last)
                for kl in range(8):
                    k = 8 * kg + kl
                    for fo in range(4):
                        for tt in range(2):
                            def m1(e, half=half, kl=kl, fo=fo, tt=tt, k=k):
                                return e.matmul(
                                    banks[fo * 2 + tt][0:128, :],
                                    bass.AP(w1_buf, half * H1 + kl * 512 + fo * 128,
                                            [[2 * H1, 128], [1, 128]]),
                                    d_fm[:, T * k + 512 * tt:T * k + 512 * tt + 512],
                                    start=(k == 0), stop=(k == 63))
                            if kl == 7 and fo == 3 and tt == 1:
                                w_m1 = S.op("tensor", m1, mm)
                            else:
                                S.op("tensor", m1)
                slab_mm1.append(mm.v)
            S.wait("vector", mm, w_m1)
            for fo in range(4):
                f_g = 4 * ps + fo
                S.wait("vector", gst[f_g % 2], sto11.get(f_g - 2, 0))
                for tt in range(2):
                    w_e = S.op("vector", _f(lambda e, fo=fo, tt=tt, f_g=f_g: e.tensor_scalar_add(
                        estg[fo % 2][:, 512 * tt:512 * (tt + 1)],
                        banks[fo * 2 + tt][0:128, :],
                        b1_cols[:, f_g:f_g + 1])), vv)
                S.wait("sync", vv, w_e)
                sto11[f_g] = S.dma("sync", gst[f_g % 2], e_st[128 * f_g:128 * (f_g + 1), :],
                                   estg[fo % 2][:, :])
            S.wait("tensor", vv, vv.v)
            prev_pass_mm = mm.v
        S.wait("sync", gst[0], gst[0].v)
        S.wait("sync", gst[1], gst[1].v)
        S.emit("p11")
        emit_dram_tap("t_e", e_st, DM, T)
        p11.close()
        p10.close()
        pw0.close()

    if BUILD_UPTO >= 12:
        p13 = ExitStack()
        etm = [p13.enter_context(nc.sbuf_tensor(f"etm{i}", [128, DM], F16)) for i in range(2)]
        x2r = [p13.enter_context(nc.sbuf_tensor(f"x2r{i}", [128, DM], F32)) for i in range(2)]
        nrm3 = p13.enter_context(nc.sbuf_tensor("nrm3", [128, DM], F32))
        outt = p13.enter_context(nc.sbuf_tensor("outt", [128, DM], F32))
        scr13 = p13.enter_context(nc.sbuf_tensor("scr13", [128, DM], F16))
        cl13 = p13.enter_context(nc.sbuf_tensor("cl13", [128, 8], F32))
        st13 = {}
        dve13 = {}
        for i in range(8):
            if i >= 2:
                S.wait("sync", vv, dve13.get(i - 2, 0))
            w_e = S.dma("sync", xq[i % 2], etm[i % 2][:, :], e_st[:, CH * i:CH * (i + 1)], transpose=True)
            w_x = S.dma("sync", xq[i % 2], x2r[i % 2][:, :], x2_st[CH * i:CH * (i + 1), :])
            S.wait("scalar", xq[i % 2], w_e)
            w_s1 = S.op("scalar", _f(lambda e, i=i: e.activation(
                scr13[:, :], etm[i % 2][:, :], AF.Square, accum_out=cl13[:, 1:2])), ssc)
            S.wait("vector", xq[i % 2], w_e)
            w_rs13 = S.op("vector", _f(lambda e, i=i: e.reduce_sum(cl13[:, 0:1], etm[i % 2][:, :], axis=AX.X)), vv)
            S.wait("vector", vv, w_rs13)
            S.op("vector", _f(lambda e: e.tensor_single_scalar(cl13[:, 2:3], cl13[:, 0:1], 1.0 / DM, ALU.mult)))
            S.wait("vector", ssc, w_s1)
            S.op("vector", _f(lambda e: e.tensor_single_scalar(cl13[:, 3:4], cl13[:, 1:2], 1.0 / DM, ALU.mult)))
            w_m213 = S.op("vector", _f(lambda e: e.tensor_tensor(cl13[:, 4:5], cl13[:, 2:3], cl13[:, 2:3], ALU.mult)), vv)
            S.wait("vector", vv, w_m213)
            w_v = S.op("vector", _f(lambda e: e.tensor_tensor(cl13[:, 4:5], cl13[:, 3:4], cl13[:, 4:5], ALU.subtract)), vv)
            S.wait("scalar", vv, w_v)
            w_sq13 = S.op("scalar", _f(lambda e: e.activation(cl13[:, 5:6], cl13[:, 4:5], AF.Sqrt, bias=eps_col[:, 0:1])), ssc)
            S.wait("vector", ssc, w_sq13)
            w_rc13 = S.op("vector", _f(lambda e: e.reciprocal(cl13[:, 6:7], cl13[:, 5:6])), vv)
            S.wait("vector", vv, w_rc13)
            S.op("vector", _f(lambda e, i=i: e.tensor_scalar(
                nrm3[:, :], etm[i % 2][:, :], cl13[:, 2:3], cl13[:, 6:7], ALU.subtract, ALU.mult)))
            S.wait("vector", xq[i % 2], w_x)
            S.wait("vector", st, st.v)
            w_o = S.op("vector", _f(lambda e, i=i: e.tensor_add(
                outt[:, :], x2r[i % 2][:, :], nrm3[:, :])), vv)
            S.wait("sync", vv, w_o)
            dve13[i] = w_o
            st13[i] = S.dma("sync", st, y_out[CH * i:CH * (i + 1), :], outt[:, :])
        S.wait("sync", st, st.v)
        S.emit("p13")
        p13.close()

    return nc


def kernel(L0_attn_hidden_states, L0_attn_in_proj_w, L0_attn_conv_w, L0_attn_conv_b,
           L0_attn_dt_bias, L0_attn_A_log, L0_attn_D, L0_attn_norm_w,
           L0_attn_out_proj_w, L0_ffn_W0, L0_ffn_bias0, L0_ffn_W1, L0_ffn_bias1):
    x = np.asarray(L0_attn_hidden_states, dtype=np.float32)
    w_in_T = np.ascontiguousarray(np.asarray(L0_attn_in_proj_w, np.float32).T)
    w_out_T = np.ascontiguousarray(np.asarray(L0_attn_out_proj_w, np.float32).T)
    w0_T = np.ascontiguousarray(np.asarray(L0_ffn_W0, np.float32).T)
    w1_T = np.ascontiguousarray(np.asarray(L0_ffn_W1, np.float32).T)

    nc = build()
    in_maps = []
    for core in range(8):
        b, h = core // 2, core % 2
        shard = np.ascontiguousarray(x[b, T * h:T * (h + 1), :])
        flags = np.array([1.0 - h, float(h)], np.float32)
        in_maps.append({
            "x_in": shard, "w_in_T": w_in_T, "w_out_T": w_out_T,
            "w0_T": w0_T, "w1_T": w1_T,
            "conv_w": np.asarray(L0_attn_conv_w, np.float32),
            "conv_b": np.asarray(L0_attn_conv_b, np.float32),
            "dt_bias": np.asarray(L0_attn_dt_bias, np.float32),
            "A_log": np.asarray(L0_attn_A_log, np.float32),
            "Dp": np.asarray(L0_attn_D, np.float32),
            "norm_w": np.asarray(L0_attn_norm_w, np.float32),
            "b0": np.asarray(L0_ffn_bias0, np.float32),
            "b1": np.asarray(L0_ffn_bias1, np.float32),
            "flags": flags,
        })
    res = run_bass_kernel_spmd(nc, in_maps, core_ids=list(range(8)))
    out = np.empty((4, 2048, DM), np.float32)
    for core in range(8):
        b, h = core // 2, core % 2
        out[b, T * h:T * (h + 1), :] = res.results[core]["y_out"]
    return out
